# version 19
# baseline (speedup 1.0000x reference)
import dataclasses
import numpy as np

import concourse.bass as bass
import concourse.mybir as mybir
from concourse.bass_utils import run_bass_kernel_spmd

F32 = mybir.dt.float32
BF16 = mybir.dt.bfloat16
FP16 = mybir.dt.float16
AF = mybir.ActivationFunctionType
ALU = mybir.AluOpType
AXX = mybir.AxisListType.X

TAPS = ()
LAST_RES = None

D = 1024
TC = 2048
ECN = 8
DCN = 8
NT5 = 4
NTB = 16
H = 4096
DECAY = 0.95
EPS = 1e-5


def _b0(ap, n):
    return dataclasses.replace(ap, ap=[list(p) for p in ap.ap[:-1]] + [[0, n]])


def build_nc():
    nc = bass.Bass("TRN2", target_bir_lowering=False)

    x_in = nc.dram_tensor("x_in", [TC, D], F32, kind="ExternalInput").ap()
    ident = nc.dram_tensor("ident", [128, 128], F32, kind="ExternalInput").ap()
    sel = nc.dram_tensor("sel", [8, 1], F32, kind="ExternalInput").ap()
    wk = [nc.dram_tensor(f"wk{i}", [ECN, 128, D], F32, kind="ExternalInput").ap() for i in range(2)]
    wv = [nc.dram_tensor(f"wv{i}", [ECN, 128, D], F32, kind="ExternalInput").ap() for i in range(2)]
    wr = [nc.dram_tensor(f"wr{i}", [ECN, 128, D], F32, kind="ExternalInput").ap() for i in range(2)]
    wo = [nc.dram_tensor(f"wo{i}", [ECN, 128, D], F32, kind="ExternalInput").ap() for i in range(2)]
    lnw = [nc.dram_tensor(f"lnw{i}", [128, 8], F32, kind="ExternalInput").ap() for i in range(2)]
    lnb = [nc.dram_tensor(f"lnb{i}", [128, 8], F32, kind="ExternalInput").ap() for i in range(2)]
    wexpT = nc.dram_tensor("wexpT", [DCN, 128, 4096], F32, kind="ExternalInput").ap()
    wconP = nc.dram_tensor("wconP", [8, 128, 4096], F32, kind="ExternalInput").ap()
    dsul = nc.dram_tensor("dsul", [7, 128, 4096], F32, kind="ExternalInput").ap()
    dsuh = nc.dram_tensor("dsuh", [5, 128, 4096], F32, kind="ExternalInput").ap()
    dsdl = nc.dram_tensor("dsdl", [7, 128, 4096], F32, kind="ExternalInput").ap()
    dsdh = nc.dram_tensor("dsdh", [5, 128, 4096], F32, kind="ExternalInput").ap()
    bexp_c = nc.dram_tensor("bexp_c", [128, 32], F32, kind="ExternalInput").ap()
    pident = nc.dram_tensor("pident", [128, 128], F32, kind="ExternalInput").ap()
    pidT = nc.dram_tensor("pidT", [128, 128], F32, kind="ExternalInput").ap()
    upb_r = nc.dram_tensor("upb_r", [128, 32], F32, kind="ExternalInput").ap()
    bcon2 = nc.dram_tensor("bcon2", [2, D], F32, kind="ExternalInput").ap()
    y_out = nc.dram_tensor("y_out", [TC, D], F32, kind="ExternalOutput").ap()

    m1p = nc.dram_tensor("m1p", [DCN, 128, H], BF16).ap()
    uhp = nc.dram_tensor("uhp", [128, H], BF16).ap()
    m2d = nc.dram_tensor("m2d", [H, D], BF16).ap()
    wcf = nc.dram_tensor("wcf", [H, D], BF16).ap()
    n2td = nc.dram_tensor("n2td", [128, 16384], BF16).ap()
    x1d = nc.dram_tensor("x1d", [128, 16384], F32).ap()
    cc_in = [nc.dram_tensor(f"cc_in{i}", [1, D], F32).ap() for i in range(2)]
    cc_out = [nc.dram_tensor(f"cc_out{i}", [8, D], F32).ap() for i in range(2)]

    B0 = ((nc.sbuf_base + 31) // 32) * 32

    def at(name, shape, dt, off):
        return nc.alloc_sbuf_tensor_at(name, shape, dt, offset=B0 + off)

    BIG1 = at("big1", [128, 16384], F32, 0)
    WUP = at("wup", [128, 32768], BF16, 0)
    BIG2 = at("big2", [128, 16384], FP16, 65536)
    GBUF = at("gbuf", [128, 16384], BF16, 65536)
    FST = [at("fst0", [128, 4096], F32, 65536),
           at("fst1", [128, 4096], F32, 81920)]
    FSTN = at("fstn", [128, 16384], BF16, 65536)
    BIG3 = at("big3", [128, 16384], BF16, 98304)
    WCONA = at("wcona", [128, 16384], BF16, 98304)
    BIG4 = at("big4", [128, 16384], BF16, 131072)
    WCONB = at("wconb", [128, 16384], BF16, 131072)
    wobA = at("wobA", [128, 4096], BF16, 163840)
    dsb = at("dsbuf", [128, 4096], BF16, 163840)
    wobB = at("wobB", [128, 4096], BF16, 172032)
    wxs = at("wxslot", [128, 1024], BF16, 172032)
    wcs = at("wcslot", [128, 2048], BF16, 174080)
    fstg = at("fstg", [128, 1024], BF16, 178176)
    OSTG = at("ostg", [128, 2048], F32, 163840)
    g8sb = at("g8sb", [8, D], F32, 188416)
    rowb = at("rowb", [1, D], F32, 192512)
    N2TS = at("n2ts", [128, 4096], BF16, 172032)
    SC = at("scbuf", [128, TC], F32, 180224)
    UH = at("uh", [128, 4096], BF16, 180224)
    wkb = at("wkb", [128, 2048], BF16, 188416)
    bcon_sb = at("bcon_sb", [2, D], F32, 188416)
    wvb = at("wvb", [128, 2048], BF16, 192512)
    ones2f = at("ones2f", [2, 128], F32, 192512)
    h1buf = at("h1buf", [128, 1024], BF16, 193536)
    wrb = at("wrb", [128, 2048], BF16, 196608)
    sqscr = at("sqscr", [128, 1024], BF16, 200704)
    xre = at("xre", [128, 1024], F32, 200704)
    rsl2 = at("rslot2", [128, 1024], F32, 204800)
    ST = {k: at("st_" + k, [128, NTB], F32, 208896 + j * 64) for j, k in
          enumerate(("sums", "sumsq", "mu", "ex2", "t1", "istd", "negmu"))}
    idf = at("idf", [128, 128], F32, 209344)
    idb = at("idb", [128, 128], BF16, 209856)
    dec1 = at("dec1", [128, 1], F32, 210112)
    sels = at("sels", [8, 1], F32, 210144)
    b1sb = at("b1sb", [128, 32], F32, 210176)
    b1tmp = at("b1tmp", [128, 32], F32, 210304)
    upb_sb = at("upb_sb", [128, 32], F32, 210432)
    lnw_sb = [at(f"lnw{i}s", [128, 8], F32, 210560 + i * 32) for i in range(2)]
    lnb_sb = [at(f"lnb{i}s", [128, 8], F32, 210624 + i * 32) for i in range(2)]
    ccols = at("ccols", [128, 8], F32, 210688)
    ccarry = at("ccarry", [128, 8], F32, 210720)
    pid_bf = at("pid_bf", [128, 128], BF16, 210752)
    pidT_f = at("pidT_f", [128, 128], F32, 211008)
    b1tmp2 = at("b1tmp2", [128, 32], F32, 211520)

    P = [nc.alloc_psum_tensor(f"p{j}", [128, 512], F32) for j in range(6)]
    ptrA = nc.alloc_psum_tensor("ptrA", [128, 1024], BF16)
    ptrB = nc.alloc_psum_tensor("ptrB", [128, 1024], BF16)

    ES = {"t": [], "v": [], "s": [], "g": [], "y": []}
    comp_names = ("pe", "v", "s", "cc", "ph")
    SEM = {n: nc.alloc_semaphore("s_" + n) for n in comp_names}
    cnt = {n: 0 for n in comp_names}

    def emit(eng, fn, inc=None, by=1):
        if eng == "v" and inc is None:
            inc = "v"
        if eng == "s" and inc is None:
            inc = "s"
        if eng in ("v", "s"):
            ES[eng].append(lambda e, s2=SEM[eng == "s" and "s" or "v"],
                           t=cnt["s" if eng == "s" else "v"]:
                           e.wait_ge(s2, t) if t else None)
        if inc is None:
            ES[eng].append(fn)
            return None
        cnt[inc] += by
        sem = SEM[inc]
        ES[eng].append(lambda e, f=fn, s2=sem, b=by: f(e).then_inc(s2, b))
        return cnt[inc]

    def wait(eng, sname, tgt):
        if tgt:
            ES[eng].append(lambda e, s2=SEM[sname], t=tgt: e.wait_ge(s2, t))

    class Q:

        def __init__(self, eng, name, depth=2):
            self.eng = eng
            self.sems = [nc.alloc_semaphore(f"q_{name}{j}") for j in range(depth)]
            self.tot = [0] * depth
            self.cur = 0
            self.opened = False

        def open(self):
            if not self.opened:
                j = self.cur
                if self.tot[j] > 0:
                    ES[self.eng].append(
                        lambda e, s2=self.sems[j], t=self.tot[j]: e.wait_ge(s2, t))
                self.opened = True
            return self.cur

        def dma(self, out, in_):
            j = self.open()
            self.tot[j] += 16
            ES[self.eng].append(
                lambda e, o=out, i=in_, s2=self.sems[j]:
                e.dma_start(out=o, in_=i).then_inc(s2, 16))
            return (j, self.tot[j])

        def close(self):
            j = self.cur
            tok = (j, self.tot[j])
            self.opened = False
            self.cur = (self.cur + 1) % len(self.sems)
            return tok

        def wait(self, eng, tok):
            if tok is None:
                return
            j, t = tok
            if t > 0:
                ES[eng].append(lambda e, s2=self.sems[j], t2=t: e.wait_ge(s2, t2))

        def wait_all(self, eng):
            for j in range(len(self.sems)):
                if self.tot[j] > 0:
                    ES[eng].append(
                        lambda e, s2=self.sems[j], t2=self.tot[j]: e.wait_ge(s2, t2))

    qy = Q("y", "ld")
    qw = Q("g", "w", 4)
    qo = Q("g", "o", 8)
    qo2 = Q("s", "o2", 8)
    qc = Q("g", "c", 1)
    qx = Q("g", "x")
    qr = Q("g", "r")
    qs = Q("g", "st")

    def barrier():
        cnt["ph"] += 5
        t = cnt["ph"]
        for k in ES:
            ES[k].append(lambda e, tt=t: (e.sem_inc(SEM["ph"], 1),
                                          e.wait_ge(SEM["ph"], tt)))

    def tap(name, buf, dt, cols, sem="v"):
        if name not in TAPS:
            return
        tp = nc.dram_tensor("tap_" + name, [buf.shape[0], cols], dt,
                            kind="ExternalOutput").ap()
        wait("g", sem, cnt[sem])
        qs.dma(tp, buf[:, 0:cols])
        tok = qs.close()
        for eng in ("t", "v", "s", "y"):
            qs.wait(eng, tok)

    qw.open()
    for tb in range(NTB):
        qw.dma(BIG1[:, tb * D:(tb + 1) * D], x_in[tb * 128:(tb + 1) * 128, :])
    g_x = qw.close()
    qy.dma(idf[:], ident)
    qy.dma(sels[:], sel)
    for i in range(2):
        qy.dma(lnw_sb[i][:], lnw[i])
        qy.dma(lnb_sb[i][:], lnb[i])
    qy.dma(upb_sb[:], upb_r)
    qy.dma(pidT_f[:], pidT)
    qy.dma(xre[:, 0:128], pident)
    g_p0 = qy.close()
    qy.wait("v", g_p0)
    emit("v", lambda e: e.tensor_copy(idb[:], idf[:]), "v")
    emit("v", lambda e: e.tensor_copy(pid_bf[:], xre[:, 0:128]), "v")
    emit("v", lambda e: e.memset(dec1[:], DECAY), "v")
    t_consts = cnt["v"]
    qy.wait("s", g_p0)
    qy.wait("t", g_p0)
    qy.wait("g", g_p0)
    qw.wait("v", g_x)
    qw.wait("s", g_x)
    qw.wait("t", g_x)
    qw.wait("g", g_x)

    def layer_norm(src_big, dst_big):
        wait("v", "s", cnt["s"])
        wait("s", "v", cnt["v"])
        wait("s", "pe", cnt["pe"])
        for tb in range(NTB):
            sl = src_big[:, tb * D:(tb + 1) * D]
            emit("v", lambda e, s2=sl, o=ST["sums"][:, tb:tb + 1]:
                 e.reduce_sum(o, s2, axis=AXX), "v")
            wait("s", "s", cnt["s"])
            emit("s", lambda e, s2=sl, o=ST["sumsq"][:, tb:tb + 1]:
                 e.activation(sqscr[:], s2, AF.Square, accum_out=o), "s")
        t_st_s = cnt["s"]
        wait("v", "s", t_st_s)
        emit("v", lambda e: e.tensor_scalar_mul(ST["mu"][:], ST["sums"][:], 1.0 / D))
        emit("v", lambda e: e.tensor_scalar_mul(ST["ex2"][:], ST["sumsq"][:], 1.0 / D))
        emit("v", lambda e: e.tensor_copy(ST["t1"][:], ST["mu"][:]))
        emit("v", lambda e: e.tensor_mul(ST["t1"][:], ST["t1"][:], ST["mu"][:]))
        emit("v", lambda e: e.tensor_sub(ST["ex2"][:], ST["ex2"][:], ST["t1"][:]))
        emit("v", lambda e: e.tensor_scalar_add(ST["ex2"][:], ST["ex2"][:], EPS),
             "v")
        t_varp = cnt["v"]
        wait("s", "v", t_varp)
        emit("s", lambda e: e.activation(ST["t1"][:], ST["ex2"][:], AF.Sqrt), "s")
        t_sq = cnt["s"]
        wait("v", "s", t_sq)
        emit("v", lambda e: e.reciprocal(ST["istd"][:], ST["t1"][:]))
        emit("v", lambda e: e.tensor_copy(ST["t1"][:], ST["istd"][:]))
        emit("v", lambda e: e.tensor_mul(ST["t1"][:], ST["t1"][:], ST["istd"][:]))
        emit("v", lambda e: e.tensor_mul(ST["t1"][:], ST["t1"][:], ST["ex2"][:]))
        emit("v", lambda e: e.tensor_scalar(ST["t1"][:], ST["t1"][:], -0.5, 1.5,
                                            ALU.mult, ALU.add))
        emit("v", lambda e: e.tensor_mul(ST["istd"][:], ST["istd"][:], ST["t1"][:]))
        emit("v", lambda e: e.tensor_mul(ST["negmu"][:], ST["mu"][:], ST["istd"][:]))
        emit("v", lambda e: e.tensor_scalar_mul(ST["negmu"][:], ST["negmu"][:],
                                                -1.0), "v")
        t_istd = cnt["v"]
        wait("s", "v", t_istd)
        t_aff = []
        for tb in range(NTB):
            sl = src_big[:, tb * D:(tb + 1) * D]
            dl = dst_big[:, tb * D:(tb + 1) * D]
            t_aff.append(emit("s", lambda e, s2=sl, d2=dl, tt=tb:
                              e.activation(d2, s2, AF.Identity,
                                           bias=ST["negmu"][:, tt:tt + 1],
                                           scale=ST["istd"][:, tt:tt + 1]), "s"))
        return t_aff

    def transpose_normed(t_aff, layer=None, src=None):
        if src is None:
            src = BIG4
        nb = 0
        for dc in range(DCN):
            for oct_ in range(2):
                pt = ptrA if nb % 2 == 0 else ptrB
                ev = "v" if nb % 2 == 0 else "s"
                nb += 1
                wait("t", "s", t_aff[oct_ * 8 + 7])
                wait("t", "v", cnt["v"])
                wait("t", "s", cnt["s"])
                for j in range(8):
                    tb = oct_ * 8 + j
                    emit("t", lambda e, o=pt[:, j * 128:(j + 1) * 128],
                         i2=src[:, tb * D + dc * 128: tb * D + (dc + 1) * 128]:
                         e.transpose(o, i2, idb[:]), "pe")
                wait(ev, "pe", cnt["pe"])
                dst = BIG3[:, dc * TC + oct_ * 1024: dc * TC + (oct_ + 1) * 1024]
                if layer is None:
                    if ev == "v":
                        emit("v", lambda e, d2=dst, p2=pt:
                             e.tensor_copy(d2, p2[:, 0:1024]), "v")
                    else:
                        emit("s", lambda e, d2=dst, p2=pt:
                             e.activation(d2, p2[:, 0:1024], AF.Identity), "s")
                else:
                    if ev == "v":
                        emit("v", lambda e, d2=dst, p2=pt, ii=layer, dd=dc:
                             e.tensor_scalar(d2, p2[:, 0:1024],
                                             lnw_sb[ii][:, dd:dd + 1],
                                             lnb_sb[ii][:, dd:dd + 1],
                                             ALU.mult, ALU.add), "v")
                    else:
                        emit("s", lambda e, d2=dst, p2=pt, ii=layer, dd=dc:
                             e.activation(d2, p2[:, 0:1024], AF.Identity,
                                          bias=lnb_sb[ii][:, dd:dd + 1],
                                          scale=lnw_sb[ii][:, dd:dd + 1]), "s")
        return (cnt["v"], cnt["s"])

    HSTG = [None, None]

    def chain_build(src, order, dst_col0, state, rhs0=None):
        nst = len(order)
        CH = [[BIG4[:, 8192 + bg * 2048 + h2 * 1024:
                    8192 + bg * 2048 + (h2 + 1) * 1024] for h2 in range(2)]
              for bg in range(4)]
        pairs = [(P[0], P[1]), (P[2], P[3]), (P[4], P[5])]
        t_evbg = [(cnt["v"], cnt["s"])] * 4
        t_pair = [(cnt["v"], cnt["s"])] * 3
        for fn in state["hwait"]:
            fn()
        state["hwait"] = []
        for s in range(nst):
            half = state["htog"]
            state["htog"] ^= 1
            fv, fs = state["fring"][half]
            wait("y", "v", fv)
            wait("y", "s", fs)
            qy.open()
            qy.dma(FST[half][:, 0:2048], src[order[s]][:, 0:2048])
            ty = qy.close()
            wait("s", "v", fv)
            qo2.open()
            qo2.dma(FST[half][:, 2048:4096], src[order[s]][:, 2048:4096])
            ts = qo2.close()
            qy.wait("v", ty)
            qo2.wait("s", ts)
            wait("v", "pe", state["hring"][half])
            wait("s", "pe", state["hring"][half])
            emit("v", lambda e, d=HSTG[half][:, 0:2048],
                 s3=FST[half][:, 0:2048]: e.tensor_copy(d, s3), "v")
            emit("s", lambda e, d=HSTG[half][:, 2048:4096],
                 s3=FST[half][:, 2048:4096]:
                 e.activation(d, s3, AF.Identity), "s")
            state["fring"][half] = (cnt["v"], cnt["s"])
            wait("t", "v", cnt["v"])
            wait("t", "s", cnt["s"])
            for bg in range(4):
                dslc = HSTG[half][:, bg * 1024:(bg + 1) * 1024]
                slot = bg % 3
                pa, pb = pairs[slot]
                wait("t", "v", t_pair[slot][0])
                wait("t", "s", t_pair[slot][1])
                if s > 0:
                    wait("t", "v", t_evbg[bg][0])
                    wait("t", "s", t_evbg[bg][1])
                for b8 in range(8):
                    rhs = (rhs0 if rhs0 is not None else idb[:]) if s == 0 else \
                        CH[bg][(s - 1) % 2][:, b8 * 128:(b8 + 1) * 128]
                    pj = pa if b8 < 4 else pb
                    emit("t", lambda e,
                         o=pj[:, (b8 % 4) * 128:(b8 % 4) * 128 + 128],
                         l2=dslc[:, b8 * 128:(b8 + 1) * 128], r=rhs:
                         e.matmul(o, l2, r, start=True, stop=True), "pe")
                wait("v", "pe", cnt["pe"])
                wait("s", "pe", cnt["pe"])
                if s < nst - 1:
                    d0, d1 = CH[bg][s % 2][:, 0:512], CH[bg][s % 2][:, 512:1024]
                else:
                    base = dst_col0 + bg * 1024
                    d0 = BIG3[:, base:base + 512]
                    d1 = BIG3[:, base + 512:base + 1024]
                emit("v", lambda e, d=d0, p2=pa: e.tensor_copy(d, p2[:]), "v")
                emit("s", lambda e, d=d1, p2=pb:
                     e.activation(d, p2[:], AF.Identity), "s")
                t_evbg[bg] = (cnt["v"], cnt["s"])
                t_pair[slot] = (cnt["v"], cnt["s"])
            state["hring"][half] = cnt["pe"]
        state["tdone"] = (cnt["v"], cnt["s"])

    wait("t", "v", t_consts)
    ch_state = {"hring": [0, 0], "htog": 0, "hwait": [], "tdone": (0, 0),
                "fring": [(0, 0), (0, 0)]}
    HSTG[0] = BIG4[:, 0:4096]
    HSTG[1] = BIG4[:, 4096:8192]
    chain_build(dsul, [6, 5, 4, 3, 2, 1, 0], 0, ch_state, rhs0=pid_bf[:])
    t_aprod, t_aprod_s = ch_state["tdone"]
    t_ln0 = layer_norm(BIG1, BIG1)


    t_xr2 = [0, ch_state["hring"][1]]
    t_m1ev = [cnt["v"]] * 3
    tok_m1 = None
    BIG4v = BIG4[:, 0:4096].rearrange("p (l5 b56 cc2) -> p l5 b56 cc2",
                                      l5=32, b56=4, cc2=32)
    WXB = [dsb[:, 0:4096], BIG4[:, 4096:8192]]
    fring = ch_state["fring"]
    for dc in range(DCN):
        par = dc % 2
        fv, fs = fring[par]
        wait("y", "v", fv)
        wait("y", "s", fs)
        qy.open()
        qy.dma(FST[par][:, 0:2048], wexpT[dc][:, 0:2048])
        ty = qy.close()
        wait("s", "v", fv)
        qo2.open()
        qo2.dma(FST[par][:, 2048:4096], wexpT[dc][:, 2048:4096])
        ts = qo2.close()
        qy.wait("v", ty)
        qo2.wait("s", ts)
        wait("v", "pe", t_xr2[par])
        wait("s", "pe", t_xr2[par])
        emit("v", lambda e, d=WXB[par][:, 0:2048],
             s3=FST[par][:, 0:2048]: e.tensor_copy(d, s3), "v")
        emit("s", lambda e, d=WXB[par][:, 2048:4096],
             s3=FST[par][:, 2048:4096]:
             e.activation(d, s3, AF.Identity), "s")
        fring[par] = (cnt["v"], cnt["s"])
        wait("t", "v", cnt["v"])
        wait("t", "s", cnt["s"])
        for c4 in range(8):
            slot = c4 % 3
            pj = P[2 + slot]
            wait("t", "v", t_m1ev[slot])
            for q in range(4):
                c = c4 * 4 + q
                emit("t", lambda e, o=pj[:, q * 128:(q + 1) * 128],
                     l2=WXB[par][:, c * 128:(c + 1) * 128],
                     r2=BIG3[:, c * 128:(c + 1) * 128]:
                     e.matmul(o, l2, r2, start=True, stop=True), "pe")
            wait("v", "pe", cnt["pe"])
            qo.wait("v", tok_m1)
            c0 = c4 * 4
            srcv = pj[:].rearrange("p (cq l5 b56) -> p l5 b56 cq",
                                   cq=4, l5=32, b56=4)
            dstv = BIG4v[:, :, :, c0:c0 + 4]
            emit("v", lambda e, s2=srcv, d2=dstv: e.tensor_copy(d2, s2), "v")
            t_m1ev[slot] = cnt["v"]
        t_xr2[par] = cnt["pe"]
        wait("g", "v", cnt["v"])
        qo.open()
        qo.dma(m1p[dc], BIG4[:, 0:4096])
        tok_m1 = qo.close()

    ch_state["hring"] = [cnt["pe"], cnt["pe"]]
    ch_state["hwait"] = [lambda: qo.wait("v", tok_m1),
                         lambda: qo.wait("s", tok_m1)]

    chain_build(dsuh, [4, 3, 2, 1, 0], 4096, ch_state)
    wait("g", "v", cnt["v"])
    wait("s", "v", cnt["v"])
    qo2.open()
    qo2.dma(uhp, BIG3[:, 4096:8192])
    tok_uh = qo2.close()

    chain_build(dsdh, [0, 1, 2, 3, 4], 8192, ch_state)
    tok_f = None
    t_wcpar = list(ch_state["hring"])
    t_stage = [ch_state["tdone"], ch_state["tdone"]]
    tok_out = [None, None, None, None]
    WXB2 = [BIG4[:, 0:4096], BIG4[:, 4096:8192]]
    for gb in range(8):
        par = gb % 2
        fv, fs = fring[par]
        wait("y", "v", fv)
        wait("y", "s", fs)
        qy.open()
        qy.dma(FST[par][:, 0:2048], wconP[gb][:, 0:2048])
        ty = qy.close()
        wait("s", "v", fv)
        qo2.open()
        qo2.dma(FST[par][:, 2048:4096], wconP[gb][:, 2048:4096])
        ts = qo2.close()
        qy.wait("v", ty)
        qo2.wait("s", ts)
        wait("v", "pe", t_wcpar[par])
        wait("s", "pe", t_wcpar[par])
        if gb == 0:
            qo.wait("v", tok_m1)
            qo.wait("s", tok_m1)
        emit("v", lambda e, d=WXB2[par][:, 0:2048],
             s3=FST[par][:, 0:2048]: e.tensor_copy(d, s3), "v")
        emit("s", lambda e, d=WXB2[par][:, 2048:4096],
             s3=FST[par][:, 2048:4096]:
             e.activation(d, s3, AF.Identity), "s")
        fring[par] = (cnt["v"], cnt["s"])
        wait("t", "v", cnt["v"])
        wait("t", "s", cnt["s"])
        for q in range(4):
            g = gb * 4 + q
            slot = g % 2
            pa, pb = (P[2], P[3]) if slot == 0 else (P[4], P[5])
            wait("t", "v", t_stage[slot][0])
            wait("t", "s", t_stage[slot][1])
            lh = BIG3[:, 8192 + g * 128:8192 + (g + 1) * 128]
            emit("t", lambda e, l2=lh, w2=WXB2[par], b2=q * 1024, p2=pa:
                 e.matmul(p2[:], l2, w2[:, b2:b2 + 512],
                          start=True, stop=True), "pe")
            emit("t", lambda e, l2=lh, w2=WXB2[par], b2=q * 1024, p2=pb:
                 e.matmul(p2[:], l2, w2[:, b2 + 512:b2 + 1024],
                          start=True, stop=True), "pe")
            wait("v", "pe", cnt["pe"])
            wait("s", "pe", cnt["pe"])
            stg = dsb[:, (g % 4) * 1024:(g % 4 + 1) * 1024]
            if tok_out[g % 4] is not None:
                tq_, tk_ = tok_out[g % 4]
                tq_.wait("v", tk_)
                tq_.wait("s", tk_)
            emit("v", lambda e, d=stg[:, 0:512], p2=pa:
                 e.tensor_copy(d, p2[:]), "v")
            emit("s", lambda e, d=stg[:, 512:1024], p2=pb:
                 e.activation(d, p2[:], AF.Identity), "s")
            t_stage[slot] = (cnt["v"], cnt["s"])
            wait("g", "v", cnt["v"])
            wait("g", "s", cnt["s"])
            qo.open()
            qo.dma(m2d[g * 128:(g + 1) * 128, :], stg)
            tok_out[g % 4] = (qo, qo.close())
        t_wcpar[par] = cnt["pe"]
    ch_state["hring"] = [t_wcpar[0], t_wcpar[1]]

    chain_build(dsdl, [0, 1, 2, 3, 4, 5, 6], 12288, ch_state)
    m2r = m2d.rearrange("(l cc2) d -> cc2 l d", l=128, cc2=32)
    wcfr = wcf.rearrange("(l5 b56 cc2) d -> cc2 b56 l5 d", l5=32, b56=4, cc2=32)
    qo.wait_all("y")
    qo.wait_all("g")
    t_wcpar = list(ch_state["hring"])
    t_stage = [ch_state["tdone"], ch_state["tdone"]]
    for cb in range(8):
        par = cb % 2
        wait("y", "pe", t_wcpar[par])
        wait("g", "pe", t_wcpar[par])
        qy.open()
        qy.dma(WXB2[par][:, 0:2048], m2r[cb * 4:cb * 4 + 2]
               .rearrange("q p e -> p q e"))
        gtok = qy.close()
        qw.open()
        qw.dma(WXB2[par][:, 2048:4096], m2r[cb * 4 + 2:cb * 4 + 4]
               .rearrange("q p e -> p q e"))
        gtok2 = qw.close()
        qy.wait("t", gtok)
        qw.wait("t", gtok2)
        for q in range(4):
            c = cb * 4 + q
            slot = c % 2
            pa, pb = (P[2], P[3]) if slot == 0 else (P[4], P[5])
            wait("t", "v", t_stage[slot][0])
            wait("t", "s", t_stage[slot][1])
            lh = BIG3[:, 12288 + c * 128:12288 + (c + 1) * 128]
            emit("t", lambda e, l2=lh, w2=WXB2[par], b2=q * 1024, p2=pa:
                 e.matmul(p2[:], l2, w2[:, b2:b2 + 512],
                          start=True, stop=True), "pe")
            emit("t", lambda e, l2=lh, w2=WXB2[par], b2=q * 1024, p2=pb:
                 e.matmul(p2[:], l2, w2[:, b2 + 512:b2 + 1024],
                          start=True, stop=True), "pe")
            wait("v", "pe", cnt["pe"])
            wait("s", "pe", cnt["pe"])
            stg = dsb[:, (c % 4) * 1024:(c % 4 + 1) * 1024]
            if tok_out[c % 4] is not None:
                tq_, tk_ = tok_out[c % 4]
                tq_.wait("v", tk_)
                tq_.wait("s", tk_)
            emit("v", lambda e, d=stg[:, 0:512], p2=pa:
                 e.tensor_copy(d, p2[:]), "v")
            emit("s", lambda e, d=stg[:, 512:1024], p2=pb:
                 e.activation(d, p2[:], AF.Identity), "s")
            t_stage[slot] = (cnt["v"], cnt["s"])
            wait("s", "v", cnt["v"])
            qo2.open()
            qo2.dma(wcfr[c], stg)
            tok_f = qo2.close()
            tok_out[c % 4] = (qo2, tok_f)
        t_wcpar[par] = cnt["pe"]

    wait("g", "v", t_aprod)
    qw.open()
    qw.dma(fstg[:, 0:32], bexp_c)
    g_bexp = qw.close()
    qw.wait("t", g_bexp)
    wait("t", "v", t_aprod)
    wait("t", "s", t_aprod_s)
    wait("t", "v", cnt["v"])
    wait("t", "s", cnt["s"])
    for c in range(32):
        emit("t", lambda e, cc2=c:
             e.matmul(P[4][:, cc2:cc2 + 1], BIG3[:, cc2 * 128:(cc2 + 1) * 128],
                      fstg[:, cc2:cc2 + 1], start=True, stop=True), "pe")
    wait("v", "pe", cnt["pe"])
    emit("v", lambda e: e.tensor_copy(b1tmp[:], P[4][:, 0:32]), "v")
    wait("t", "v", cnt["v"])
    emit("t", lambda e: e.matmul(P[5][:, 0:32], pidT_f[:], b1tmp[:],
                                 start=True, stop=True), "pe")
    wait("v", "pe", cnt["pe"])
    emit("v", lambda e: e.tensor_copy(b1tmp2[:], P[5][:, 0:32]), "v")
    wait("v", "v", cnt["v"])
    emit("v", lambda e: e.transpose(b1sb[:], b1tmp2[:]), "v")

    t_aff0 = layer_norm(BIG1, FSTN)

    barrier()

    t_wo_end = 0
    for i in range(2):
        if i == 0:
            t_aff = t_aff0
            src_n = FSTN
        else:
            t_aff = layer_norm(BIG1, BIG4)
            src_n = BIG4
        t_nt, t_nt_s = transpose_normed(t_aff, layer=i, src=src_n)
        if i == 0:
            tap("nt0", BIG3, BF16, 16384)

        tok_kv = [None] * ECN
        kvpar = [None] * ECN
        t_wpar = [0, 0]
        KS = [P[0], P[1], P[2], P[3]]
        VS = [P[4], P[5]]

        def ld_kv(ec, lay=i):
            wait("g", "pe", t_wpar[qw.cur % 2])
            par = qw.open() % 2
            hf2 = par * 1024
            qw.dma(wkb[:, hf2:hf2 + 1024], wk[lay][ec])
            qw.dma(wvb[:, hf2:hf2 + 1024], wv[lay][ec])
            kvpar[ec] = par
            tok_kv[ec] = qw.close()

        ld_kv(0)
        t_k05_last = 0
        t_vkv = 0
        for ec in range(ECN):
            if ec + 1 < ECN:
                ld_kv(ec + 1)
            qw.wait("t", tok_kv[ec])
            hf = kvpar[ec] * 1024
            wait("t", "v", t_nt)
            wait("t", "s", t_nt_s)
            wait("t", "s", t_k05_last)
            for dc in range(DCN):
                for t5 in range(NT5):
                    emit("t", lambda e, o=KS[t5][:],
                         l2=wkb[:, hf + dc * 128: hf + (dc + 1) * 128],
                         r2=BIG3[:, dc * TC + t5 * 512: dc * TC + (t5 + 1) * 512],
                         a=dc == 0, b=dc == DCN - 1:
                         e.matmul(o, l2, r2, start=a, stop=b), "pe")
            t_kpass = cnt["pe"]
            wait("s", "pe", t_kpass)
            wait("s", "v", cnt["v"])
            t_k05 = []
            for t5 in range(NT5):
                t_k05.append(emit("s", lambda e, o=SC[:, t5 * 512:(t5 + 1) * 512],
                                  p2=KS[t5]: e.activation(
                                      o, p2[:], AF.Identity, scale=1.0 - DECAY),
                                  "s"))
            t_k05_last = t_k05[-1]
            for pair in range(2):
                wait("t", "v", t_vkv)
                for dc in range(DCN):
                    for m in range(2):
                        t5 = pair * 2 + m
                        emit("t", lambda e, o=VS[m][:],
                             l2=wvb[:, hf + dc * 128: hf + (dc + 1) * 128],
                             r2=BIG3[:, dc * TC + t5 * 512: dc * TC + (t5 + 1) * 512],
                             a=dc == 0, b=dc == DCN - 1:
                             e.matmul(o, l2, r2, start=a, stop=b), "pe")
                t_vp = cnt["pe"]
                for m in range(2):
                    t5 = pair * 2 + m
                    wait("v", "pe", t_vp)
                    wait("v", "s", t_k05[t5])
                    dst = BIG2[:, ec * TC + t5 * 512: ec * TC + (t5 + 1) * 512]
                    emit("v", lambda e, d2=dst, sl2=SC[:, t5 * 512:(t5 + 1) * 512],
                         p2=VS[m]: e.tensor_mul(d2, sl2, p2[:]), "v")
                t_vkv = cnt["v"]
            t_wpar[kvpar[ec]] = cnt["pe"]
            srcv = BIG2[:, ec * TC:(ec + 1) * TC]
            wait("v", "v", cnt["v"])
            emit("v", lambda e, s2=srcv: e.tensor_tensor_scan(
                SC[:], _b0(dec1[:, 0:1], TC), s2, 0.0, ALU.mult, ALU.add), "v")
            emit("v", lambda e, cc2=ec: e.tensor_copy(
                ccols[:, cc2:cc2 + 1], SC[:, TC - 1:TC]), "v")
        t_contrib = cnt["v"]
        if i == 0:
            tap("kv0", BIG2, FP16, 16384)
            tap("cc0", ccols, F32, 8)

        tok_r = [None] * ECN
        rpar = [None] * ECN
        t_rpar = [0, 0]

        def ld_r(ec, lay=i):
            wait("g", "pe", t_rpar[qr.cur])
            par = qr.open()
            qr.dma(wrb[:, par * 1024:(par + 1) * 1024], wr[lay][ec])
            rpar[ec] = par
            tok_r[ec] = qr.close()

        ld_r(0)
        ld_r(1)
        qo.wait_all("g")
        qo2.wait_all("g")
        wait("g", "pe", t_wo_end)
        tok_wo = [None] * ECN

        def ld_wo(ec, lay=i):
            qw.open()
            qw.dma(wobA[:, ec * 512:(ec + 1) * 512], wo[lay][ec][:, 0:512])
            qw.dma(wobB[:, ec * 512:(ec + 1) * 512], wo[lay][ec][:, 512:1024])
            tok_wo[ec] = qw.close()

        ld_wo(0)
        SIG = [rsl2[:, 0:512], rsl2[:, 512:1024],
               xre[:, 0:512], xre[:, 512:1024]]
        t_y_ring = [0, 0, 0, 0]
        t_sigs = [[0] * NT5 for _ in range(ECN)]
        t_rp = [0] * ECN

        def emit_rpass(ec):
            hf = rpar[ec] * 1024
            qr.wait("t", tok_r[ec])
            if ec == 0:
                wait("t", "s", t_k05_last)
            else:
                wait("t", "s", t_sigs[ec - 1][NT5 - 1])
            for dc in range(DCN):
                for t5 in range(NT5):
                    emit("t", lambda e, o=KS[t5][:],
                         l2=wrb[:, hf + dc * 128: hf + (dc + 1) * 128],
                         r2=BIG3[:, dc * TC + t5 * 512: dc * TC + (t5 + 1) * 512],
                         a=dc == 0, b=dc == DCN - 1:
                         e.matmul(o, l2, r2, start=a, stop=b), "pe")
            t_rpar[rpar[ec]] = cnt["pe"]
            t_rp[ec] = cnt["pe"]

        def emit_sigs(ec):
            for t5 in range(NT5):
                wait("s", "pe", t_rp[ec])
                wait("s", "v", t_y_ring[t5])
                t_sigs[ec][t5] = emit("s", lambda e, o=SIG[t5], p2=KS[t5]:
                                      e.activation(o, p2[:], AF.Sigmoid), "s")

        qo.wait_all("v")
        qo.wait_all("g")
        wait("t", "v", t_contrib)
        for ec in range(ECN):
            emit("t", lambda e, o=P[4][0:1, ec % 4 * 128:(ec % 4 + 1) * 128]
                 if ec < 4 else P[5][0:1, (ec - 4) * 128:(ec - 3) * 128],
                 i2=ccols[:, ec:ec + 1]: e.transpose(o, i2, idf[:]), "pe")
        wait("v", "pe", cnt["pe"])
        emit("v", lambda e: e.tensor_copy(rowb[0:1, 0:512], P[4][0:1, :]), "v")
        t_ccrow = emit("v", lambda e: e.tensor_copy(rowb[0:1, 512:1024],
                                                    P[5][0:1, :]), "v")
        wait("g", "v", t_ccrow)
        qc.dma(cc_in[i], rowb[0:1, :])
        tok_ci = qc.close()
        qc.wait("g", tok_ci)
        t_cc = emit("g", lambda e, ii=i: e.collective_compute(
            "AllGather", ALU.bypass, replica_groups=[list(range(8))],
            ins=[cc_in[ii]], outs=[cc_out[ii]]), "cc", 1)
        wait("g", "cc", t_cc)
        wait("g", "pe", cnt["pe"])
        qc.dma(g8sb[:], cc_out[i])
        tok_g8 = qc.close()
        emit_rpass(0)
        emit_sigs(0)
        emit_rpass(1)

        qc.wait("t", tok_g8)
        wait("t", "v", t_ccrow)
        emit("t", lambda e: e.matmul(P[4][0:1, :], sels[:], g8sb[:, 0:512],
                                     start=True, stop=True), "pe")
        emit("t", lambda e: e.matmul(P[5][0:1, :], sels[:], g8sb[:, 512:1024],
                                     start=True, stop=True), "pe")
        wait("v", "pe", cnt["pe"])
        emit("v", lambda e: e.tensor_copy(rowb[0:1, 0:512], P[4][0:1, :]), "v")
        t_carow = emit("v", lambda e: e.tensor_copy(rowb[0:1, 512:1024],
                                                    P[5][0:1, :]), "v")
        wait("t", "v", t_carow)
        for ec in range(ECN):
            emit("t", lambda e, o=P[4][:, ec:ec + 1],
                 i2=rowb[0:1, ec * 128:(ec + 1) * 128]:
                 e.transpose(o, i2, idf[0:1, 0:1]), "pe")
        wait("v", "pe", cnt["pe"])
        t_carry = emit("v", lambda e: e.tensor_copy(ccarry[:], P[4][:, 0:8]),
                       "v")
        if i == 0:
            tap("ca0", ccarry, F32, 8)

        for ec in range(ECN):
            if ec >= 1:
                ld_wo(ec)
            srcv = BIG2[:, ec * TC:(ec + 1) * TC]
            wait("v", "v", cnt["v"])
            emit("v", lambda e, s2=srcv, cc2=ec: e.tensor_tensor_scan(
                SC[:], _b0(dec1[:, 0:1], TC), s2,
                ccarry[:, cc2:cc2 + 1], ALU.mult, ALU.add), "v")
            if ec >= 2:
                emit_rpass(ec)
            if ec + 2 < ECN:
                ld_r(ec + 2)
            if ec >= 1:
                emit_sigs(ec)
            for t5 in range(NT5):
                wait("v", "s", t_sigs[ec][t5])
                ydst = BIG4[:, ec * TC + t5 * 512: ec * TC + (t5 + 1) * 512]
                emit("v", lambda e, d2=ydst, sg=SIG[t5], t5_=t5: e.tensor_mul(
                    d2, sg, SC[:, t5_ * 512:(t5_ + 1) * 512]), "v")
                t_y_ring[t5] = cnt["v"]
        t_y_done = cnt["v"]
        if i == 0:
            tap("y0", BIG4, BF16, 16384)

        for ec in range(ECN):
            qw.wait("t", tok_wo[ec])
        wait("t", "v", t_y_done)
        XR = [xre, rsl2]
        tok_xr2 = [None, None]
        if i == 1:
            wait("g", "s", t_aff[-1])
            wait("g", "v", t_y_done)
            for pre in range(2):
                qx.open()
                qx.dma(XR[pre][:], x_in[pre * 128:(pre + 1) * 128, :])
                tok_xr2[pre] = qx.close()
        t_ev_ring = [0, 0]
        for tb in range(NTB):
            par = tb % 2
            pa, pb = (P[4], P[5]) if par == 0 else (P[0], P[1])
            wait("t", "v", t_ev_ring[par])
            for ec in range(ECN):
                emit("t", lambda e, o=pa[:],
                     l2=BIG4[:, ec * TC + tb * 128: ec * TC + (tb + 1) * 128],
                     r2=wobA[:, ec * 512:(ec + 1) * 512],
                     a=ec == 0, b=ec == ECN - 1:
                     e.matmul(o, l2, r2, start=a, stop=b), "pe")
            for ec in range(ECN):
                emit("t", lambda e, o=pb[:],
                     l2=BIG4[:, ec * TC + tb * 128: ec * TC + (tb + 1) * 128],
                     r2=wobB[:, ec * 512:(ec + 1) * 512],
                     a=ec == 0, b=ec == ECN - 1:
                     e.matmul(o, l2, r2, start=a, stop=b), "pe")
            t_mm = cnt["pe"]
            if i == 1:
                qx.wait("v", tok_xr2[tb % 2])
                sl = BIG1[:, tb * D:(tb + 1) * D]
                emit("v", lambda e, s2=sl, xr2=XR[tb % 2]:
                     e.scalar_tensor_tensor(
                         s2, xr2[:], 1.0, s2, ALU.mult, ALU.add), "v")
                if tb + 2 < NTB:
                    wait("g", "v", cnt["v"])
                    qx.open()
                    qx.dma(XR[tb % 2][:], x_in[(tb + 2) * 128:(tb + 3) * 128, :])
                    tok_xr2[tb % 2] = qx.close()
            wait("v", "pe", t_mm)
            sl0 = BIG1[:, tb * D: tb * D + 512]
            sl1 = BIG1[:, tb * D + 512: tb * D + 1024]
            emit("v", lambda e, s2=sl0, p2=pa: e.tensor_add(s2, s2, p2[:]), "v")
            emit("v", lambda e, s2=sl1, p2=pb: e.tensor_add(s2, s2, p2[:]), "v")
            t_ev_ring[par] = cnt["v"]
        t_wo_end = cnt["pe"]
        if i == 0:
            tap("s0", BIG1, F32, 16384)

    tap("x1", BIG1, F32, 16384)
    t_big1_final = cnt["v"]
    wcfh = wcf.rearrange("(h l5 p) d -> h p l5 d", h=2, l5=16, p=128)
    qo2.wait("g", tok_f)
    wait("g", "pe", t_wo_end)
    qw.open()
    qw.dma(WCONB[:, :].rearrange("p (l5 d) -> p l5 d", l5=16), wcfh[1])
    g_wcb = qw.close()
    wait("s", "v", t_big1_final)
    qo2.open()
    qo2.dma(x1d[:, 0:8192], BIG1[:, 0:8192])
    tok_x1d = qo2.close()
    wait("y", "v", t_big1_final)
    qy.open()
    qy.dma(x1d[:, 8192:16384], BIG1[:, 8192:16384])
    tok_x1d2 = qy.close()
    t_aff = layer_norm(BIG1, FSTN)
    transpose_normed(t_aff, layer=None, src=FSTN)
    wait("g", "v", cnt["v"])
    wait("g", "s", cnt["s"])
    qo.open()
    qo.dma(n2td, BIG3[:, :])
    tok_spill = qo.close()
    barrier()

    qo.wait("y", tok_spill)
    qo.wait("v", tok_spill)
    qo.wait("g", tok_spill)
    qo.wait("s", tok_spill)
    qo2.wait("y", tok_x1d)
    qo2.wait("g", tok_x1d)
    qo2.wait("v", tok_f)
    qo.wait("y", tok_m1)
    qo.wait("g", tok_m1)
    qo2.wait("y", tok_uh)
    qo2.wait("y", tok_f)
    qo2.wait("s", tok_f)
    qy.wait("y", tok_x1d2)
    qy.wait("g", tok_x1d2)
    qy.open()
    for dc in range(4):
        qy.dma(WUP[:, dc * H:(dc + 1) * H], m1p[dc])
    qy.dma(UH[:], uhp)
    qy.dma(bcon_sb[:], bcon2)
    g_wup = qy.close()
    qw.open()
    for dc in range(4, DCN):
        qw.dma(WUP[:, dc * H:(dc + 1) * H], m1p[dc])
    g_wup2 = qw.close()
    qo2.open()
    qo2.dma(WCONA[:, :].rearrange("p (l5 d) -> p l5 d", l5=16), wcfh[0])
    g_wca = qo2.close()
    qy.wait("v", g_wup)
    emit("v", lambda e: e.memset(ones2f[:], 1.0), "v")
    t_ones = cnt["v"]
    qy.wait("t", g_wup)
    qw.wait("t", g_wup2)
    qo2.wait("t", g_wca)
    qw.wait("t", g_wcb)
    wait("t", "v", t_ones)

    XR2 = [xre, rsl2]
    tok_fx = [None, None]
    wait("g", "v", t_big1_final)
    wait("g", "s", t_aff[-1])
    for pre in range(2):
        qx.open()
        qx.dma(XR2[pre][:], x1d[:, pre * D:(pre + 1) * D])
        tok_fx[pre] = qx.close()

    st_toks = []
    t_cev = 0
    t_cev_ring = [0, 0]
    t_h1_ring = [0, 0]
    t_gelu_ring = [0, 0]
    t_exp_end = 0
    t_con_end = 0
    for t5 in range(NT5):
        if t5 >= 1:
            wait("y", "pe", t_exp_end)
        qy.open()
        for dc in range(DCN):
            qy.dma(N2TS[:, dc * 512:(dc + 1) * 512],
                   n2td[:, dc * TC + t5 * 512: dc * TC + (t5 + 1) * 512])
        tok_n2 = qy.close()
        qy.wait("t", tok_n2)
        t_gelu = []

        def emit_expand(l5):
            m = l5 % 2
            pex = P[0] if m == 0 else P[1]
            wait("t", "v", t_h1_ring[m])
            for dc in range(DCN):
                emit("t", lambda e, p2=pex,
                     l2=WUP[:, dc * H + l5 * 128: dc * H + (l5 + 1) * 128],
                     r2=N2TS[:, dc * 512:(dc + 1) * 512],
                     a=dc == 0, b=dc == DCN - 1:
                     e.matmul(p2[:], l2, r2, start=a, stop=b), "pe")
            return cnt["pe"]

        def emit_h1(l5, t_exp):
            m = l5 % 2
            pex = P[0] if m == 0 else P[1]
            wait("v", "pe", t_exp)
            t_h1 = emit("v", lambda e, p2=pex, h2=m * 512, l2=l5: e.tensor_add(
                h1buf[:, h2:h2 + 512], p2[:],
                _b0(b1sb[:, l2:l2 + 1], 512)), "v")
            t_h1_ring[m] = t_h1
            return t_h1

        def emit_ghigh_gelu(l5, t_h1, t5_):
            m = l5 % 2
            pgh = P[2] if m == 0 else P[3]
            wait("t", "v", t_h1)
            wait("t", "s", t_gelu_ring[m])
            wait("t", "v", t_cev_ring[1])
            emit("t", lambda e, l2=UH[:, l5 * 128:(l5 + 1) * 128],
                 r2=h1buf[:, m * 512:m * 512 + 512], p2=pgh:
                 e.matmul(p2[:], l2, r2, start=True, stop=True), "pe")
            t_gh = cnt["pe"]
            wait("s", "pe", t_gh)
            if t5_ >= 1:
                wait("s", "pe", t_con_end)
            t_gelu.append(emit("s", lambda e, p2=pgh, l2=l5: e.activation(
                GBUF[:, l2 * 512:(l2 + 1) * 512], p2[:], AF.Gelu,
                bias=upb_sb[:, l2:l2 + 1]), "s"))
            t_gelu_ring[m] = t_gelu[-1]

        th_prev = None
        for l5 in range(32):
            t_exp = emit_expand(l5)
            t_exp_end = t_exp
            if l5 >= 1:
                emit_ghigh_gelu(l5 - 1, th_prev, t5)
            th_prev = emit_h1(l5, t_exp)
        emit_ghigh_gelu(31, th_prev, t5)
        t_gelu_all = t_gelu[-1]

        for tq in range(4):
            tbg = t5 * 4 + tq
            pa, pb = (P[4], P[5]) if tq % 2 == 0 else (P[2], P[3])
            wait("t", "s", t_gelu_all)
            wait("t", "v", t_cev_ring[tq % 2])
            if tq % 2 == 1:
                wait("t", "s", t_gelu_all)
            emit("t", lambda e, p2=pa: e.matmul(p2[:], ones2f[:],
                                                bcon_sb[:, 0:512], start=True,
                                                stop=False), "pe")
            emit("t", lambda e, p2=pb: e.matmul(p2[:], ones2f[:],
                                                bcon_sb[:, 512:1024],
                                                start=True, stop=False), "pe")
            for l5 in range(32):
                lh = GBUF[:, l5 * 512 + tq * 128: l5 * 512 + (tq + 1) * 128]
                wt = WCONA if l5 < 16 else WCONB
                emit("t", lambda e, l2=lh, p2=pa,
                     r2=wt[:, (l5 % 16) * 1024:(l5 % 16) * 1024 + 512],
                     b=l5 == 31:
                     e.matmul(p2[:], l2, r2, start=False, stop=b), "pe")
                emit("t", lambda e, l2=lh, p2=pb,
                     r2=wt[:, (l5 % 16) * 1024 + 512:(l5 % 16) * 1024 + 1024],
                     b=l5 == 31:
                     e.matmul(p2[:], l2, r2, start=False, stop=b), "pe")
            t_con_end = cnt["pe"]
            qx.wait("v", tok_fx[tbg % 2])
            wait("v", "pe", t_con_end)
            off = (tbg % 2) * 1024
            if len(st_toks) >= 2:
                qs.wait("v", st_toks[-2])
            emit("v", lambda e, o2=off, xr=XR2[tbg % 2], p2=pa: e.tensor_add(
                OSTG[:, o2:o2 + 512], xr[:, 0:512], p2[:]), "v")
            t_ost = emit("v", lambda e, o2=off, xr=XR2[tbg % 2], p2=pb:
                         e.tensor_add(OSTG[:, o2 + 512:o2 + 1024],
                                      xr[:, 512:1024], p2[:]), "v")
            wait("g", "v", t_ost)
            qs.open()
            qs.dma(y_out[tbg * 128:(tbg + 1) * 128, :], OSTG[:, off:off + 1024])
            st_toks.append(qs.close())
            if tbg + 2 < 16:
                qx.open()
                qx.dma(XR2[tbg % 2][:], x1d[:, (tbg + 2) * D:(tbg + 3) * D])
                tok_fx[tbg % 2] = qx.close()
            t_cev_ring[tq % 2] = t_ost
            t_cev = t_ost

    qs.wait_all("g")

    with nc.Block() as block:
        @block.tensor
        def _(e):
            for f in ES["t"]:
                f(e)

        @block.vector
        def _(e):
            for f in ES["v"]:
                f(e)

        @block.scalar
        def _(e):
            for f in ES["s"]:
                f(e)

        @block.gpsimd
        def _(e):
            for f in ES["g"]:
                f(e)

        @block.sync
        def _(e):
            for f in ES["y"]:
                f(e)

    return nc


def _dense_low(w, s, transpose):
    out = np.zeros((32, 128, 128), np.float32)
    s_ = 1 << s
    i = np.arange(4096)
    i = i[(i & s_) == 0]
    k = ((i >> (s + 1)) << s) | (i & (s_ - 1))
    c = i >> 7
    r = i & 127
    r2 = r + s_
    if not transpose:
        out[c, r, r] = w[k, 0, 0]
        out[c, r, r2] = w[k, 0, 1]
        out[c, r2, r] = w[k, 1, 0]
        out[c, r2, r2] = w[k, 1, 1]
    else:
        out[c, r, r] = w[k, 0, 0]
        out[c, r2, r] = w[k, 0, 1]
        out[c, r, r2] = w[k, 1, 0]
        out[c, r2, r2] = w[k, 1, 1]
    return out


def _dense_high(w, s, transpose):
    out = np.zeros((32, 128, 128), np.float32)
    sb = 1 << (s - 7)
    cc = np.arange(32)
    cc = cc[(cc & sb) == 0]
    cc2 = cc + sb
    if not transpose:
        for l5 in range(32):
            for b56 in range(4):
                jA = cc * 128 + b56 * 32 + l5
                kA = ((jA >> (s + 1)) << s) | (jA & ((1 << s) - 1))
                out[l5, b56 * 32 + cc, b56 * 32 + cc] = w[kA, 0, 0]
                out[l5, b56 * 32 + cc, b56 * 32 + cc2] = w[kA, 0, 1]
                out[l5, b56 * 32 + cc2, b56 * 32 + cc] = w[kA, 1, 0]
                out[l5, b56 * 32 + cc2, b56 * 32 + cc2] = w[kA, 1, 1]
    else:
        for gl in range(32):
            for m in range(4):
                l = 4 * gl + m
                jA = cc * 128 + l
                kA = ((jA >> (s + 1)) << s) | (jA & ((1 << s) - 1))
                out[gl, m * 32 + cc, m * 32 + cc] = w[kA, 0, 0]
                out[gl, m * 32 + cc2, m * 32 + cc] = w[kA, 0, 1]
                out[gl, m * 32 + cc, m * 32 + cc2] = w[kA, 1, 0]
                out[gl, m * 32 + cc2, m * 32 + cc2] = w[kA, 1, 1]
    return out


def kernel(L0_attn_h, L0_attn_ln_w, L0_attn_ln_b, L0_attn_Wk, L0_attn_Wv,
           L0_attn_Wr, L0_attn_Wo, L0_ffn_W_exp, L0_ffn_b_exp,
           L0_ffn_up_weights, L0_ffn_up_bias, L0_ffn_down_weights,
           L0_ffn_W_con, L0_ffn_b_con, L0_ffn_down_bias):
    f32 = np.float32
    x = np.ascontiguousarray(np.asarray(L0_attn_h, f32))
    Wk = np.asarray(L0_attn_Wk, f32)
    Wv = np.asarray(L0_attn_Wv, f32)
    Wr = np.asarray(L0_attn_Wr, f32)
    Wo = np.asarray(L0_attn_Wo, f32)
    ln_w = np.asarray(L0_attn_ln_w, f32)
    ln_b = np.asarray(L0_attn_ln_b, f32)
    We = np.asarray(L0_ffn_W_exp, f32)
    b_exp = np.asarray(L0_ffn_b_exp, f32)
    upw = np.asarray(L0_ffn_up_weights, f32)
    upb = np.asarray(L0_ffn_up_bias, f32)
    dnw = np.asarray(L0_ffn_down_weights, f32)
    Wc = np.asarray(L0_ffn_W_con, f32)
    b_con = np.asarray(L0_ffn_b_con, f32)
    dnb = np.asarray(L0_ffn_down_bias, f32)

    def proj_tiles(W):
        WT = np.ascontiguousarray(W.T)
        return np.ascontiguousarray(
            WT.reshape(8, 128, 8, 128).transpose(2, 1, 0, 3).reshape(8, 128, D))

    def wo_tiles(W):
        return np.ascontiguousarray(W.T.reshape(8, 128, D))

    com = {
        "ident": np.eye(128, dtype=f32),
        "wexpT": np.ascontiguousarray(
            We.T.reshape(32, 128, 8, 128).transpose(2, 1, 0, 3)
            .reshape(8, 128, 4096)),
        "wconP": np.ascontiguousarray(
            Wc.reshape(32, 128, D).transpose(1, 0, 2).reshape(32, 128, D)
            .reshape(8, 4, 128, 1024).transpose(0, 2, 1, 3)
            .reshape(8, 128, 4096)),
        "dsul": np.ascontiguousarray(
            np.stack([_dense_low(upw[s], s, False) for s in range(7)])
            .transpose(0, 2, 1, 3).reshape(7, 128, 4096)),
        "dsuh": np.ascontiguousarray(
            np.stack([_dense_high(upw[s], s, False) for s in range(7, 12)])
            .transpose(0, 2, 1, 3).reshape(5, 128, 4096)),
        "dsdl": np.ascontiguousarray(
            np.stack([_dense_low(dnw[s], s, True) for s in range(7)])
            .transpose(0, 2, 1, 3).reshape(7, 128, 4096)),
        "dsdh": np.ascontiguousarray(
            np.stack([_dense_high(dnw[s], s, True) for s in range(7, 12)])
            .transpose(0, 2, 1, 3).reshape(5, 128, 4096)),
        "bexp_c": np.ascontiguousarray(b_exp.reshape(32, 128).T),
        "upb_r": np.ascontiguousarray(
            upb.reshape(32, 4, 32).transpose(1, 0, 2).reshape(128, 32)),
        "bcon2": np.ascontiguousarray(np.stack([b_con, dnb])),
    }
    pid = np.zeros((128, 128), f32)
    jn = np.arange(128)
    pid[(jn % 4) * 32 + jn // 4, jn] = 1.0
    com["pident"] = pid
    com["pidT"] = np.ascontiguousarray(pid.T)
    for i in range(2):
        com[f"wk{i}"] = proj_tiles(Wk[i])
        com[f"wv{i}"] = proj_tiles(Wv[i])
        com[f"wr{i}"] = proj_tiles(Wr[i])
        com[f"wo{i}"] = wo_tiles(Wo[i])
        com[f"lnw{i}"] = np.ascontiguousarray(ln_w[i].reshape(8, 128).T)
        com[f"lnb{i}"] = np.ascontiguousarray(ln_b[i].reshape(8, 128).T)

    nc = build_nc()
    in_maps = []
    for c in range(8):
        b, h = c // 2, c % 2
        selv = np.zeros((8, 1), f32)
        if h == 1:
            selv[c - 1, 0] = 1.0
        im = dict(com)
        im["x_in"] = np.ascontiguousarray(x[b, h * TC:(h + 1) * TC, :])
        im["sel"] = selv
        in_maps.append(im)

    res = run_bass_kernel_spmd(nc, in_maps, core_ids=list(range(8)))
    global LAST_RES
    LAST_RES = res.results
    out = np.empty((4, 4096, D), f32)
    for c in range(8):
        b, h = c // 2, c % 2
        out[b, h * TC:(h + 1) * TC, :] = res.results[c]["y_out"]
    return out


# version 20
# speedup vs baseline: 1.0474x; 1.0474x over previous
import dataclasses
import numpy as np

import concourse.bass as bass
import concourse.mybir as mybir
from concourse.bass_utils import run_bass_kernel_spmd

F32 = mybir.dt.float32
BF16 = mybir.dt.bfloat16
FP16 = mybir.dt.float16
AF = mybir.ActivationFunctionType
ALU = mybir.AluOpType
AXX = mybir.AxisListType.X

TAPS = ()
LAST_RES = None

D = 1024
TC = 2048
ECN = 8
DCN = 8
NT5 = 4
NTB = 16
H = 4096
DECAY = 0.95
EPS = 1e-5


def _b0(ap, n):
    return dataclasses.replace(ap, ap=[list(p) for p in ap.ap[:-1]] + [[0, n]])


def build_nc():
    nc = bass.Bass("TRN2", target_bir_lowering=False)

    x_in = nc.dram_tensor("x_in", [TC, D], F32, kind="ExternalInput").ap()
    ident = nc.dram_tensor("ident", [128, 128], F32, kind="ExternalInput").ap()
    sel = nc.dram_tensor("sel", [8, 1], F32, kind="ExternalInput").ap()
    wk = [nc.dram_tensor(f"wk{i}", [ECN, 128, D], F32, kind="ExternalInput").ap() for i in range(2)]
    wv = [nc.dram_tensor(f"wv{i}", [ECN, 128, D], F32, kind="ExternalInput").ap() for i in range(2)]
    wr = [nc.dram_tensor(f"wr{i}", [ECN, 128, D], F32, kind="ExternalInput").ap() for i in range(2)]
    wo = [nc.dram_tensor(f"wo{i}", [ECN, 128, D], F32, kind="ExternalInput").ap() for i in range(2)]
    lnw = [nc.dram_tensor(f"lnw{i}", [128, 8], F32, kind="ExternalInput").ap() for i in range(2)]
    lnb = [nc.dram_tensor(f"lnb{i}", [128, 8], F32, kind="ExternalInput").ap() for i in range(2)]
    wexpT = nc.dram_tensor("wexpT", [DCN, 128, 4096], F32, kind="ExternalInput").ap()
    wconP = nc.dram_tensor("wconP", [8, 128, 4096], F32, kind="ExternalInput").ap()
    dsul = nc.dram_tensor("dsul", [7, 128, 4096], F32, kind="ExternalInput").ap()
    dsuh = nc.dram_tensor("dsuh", [5, 128, 4096], F32, kind="ExternalInput").ap()
    dsdl = nc.dram_tensor("dsdl", [7, 128, 4096], F32, kind="ExternalInput").ap()
    dsdh = nc.dram_tensor("dsdh", [5, 128, 4096], F32, kind="ExternalInput").ap()
    bexp_c = nc.dram_tensor("bexp_c", [128, 32], F32, kind="ExternalInput").ap()
    pident = nc.dram_tensor("pident", [128, 128], F32, kind="ExternalInput").ap()
    pidT = nc.dram_tensor("pidT", [128, 128], F32, kind="ExternalInput").ap()
    upb_r = nc.dram_tensor("upb_r", [128, 32], F32, kind="ExternalInput").ap()
    bcon2 = nc.dram_tensor("bcon2", [2, D], F32, kind="ExternalInput").ap()
    y_out = nc.dram_tensor("y_out", [TC, D], F32, kind="ExternalOutput").ap()

    m1p = nc.dram_tensor("m1p", [DCN, 128, H], BF16).ap()
    uhp = nc.dram_tensor("uhp", [128, H], BF16).ap()
    m2d = nc.dram_tensor("m2d", [H, D], BF16).ap()
    wcf = nc.dram_tensor("wcf", [H, D], BF16).ap()
    n2td = nc.dram_tensor("n2td", [128, 16384], BF16).ap()
    x1d = nc.dram_tensor("x1d", [128, 16384], F32).ap()
    cc_in = [nc.dram_tensor(f"cc_in{i}", [1, D], F32).ap() for i in range(2)]
    cc_out = [nc.dram_tensor(f"cc_out{i}", [8, D], F32).ap() for i in range(2)]

    B0 = ((nc.sbuf_base + 31) // 32) * 32

    def at(name, shape, dt, off):
        return nc.alloc_sbuf_tensor_at(name, shape, dt, offset=B0 + off)

    BIG1 = at("big1", [128, 16384], F32, 0)
    WUP = at("wup", [128, 32768], BF16, 0)
    BIG2 = at("big2", [128, 16384], FP16, 65536)
    GBUF = at("gbuf", [128, 16384], BF16, 65536)
    FST = [at("fst0", [128, 4096], F32, 65536),
           at("fst1", [128, 4096], F32, 81920)]
    FSTN = at("fstn", [128, 16384], BF16, 65536)
    BIG3 = at("big3", [128, 16384], BF16, 98304)
    WCONA = at("wcona", [128, 16384], BF16, 98304)
    BIG4 = at("big4", [128, 16384], BF16, 131072)
    WCONB = at("wconb", [128, 16384], BF16, 131072)
    wobA = at("wobA", [128, 4096], BF16, 163840)
    dsb = at("dsbuf", [128, 4096], BF16, 163840)
    wobB = at("wobB", [128, 4096], BF16, 172032)
    wxs = at("wxslot", [128, 1024], BF16, 172032)
    wcs = at("wcslot", [128, 2048], BF16, 174080)
    fstg = at("fstg", [128, 1024], BF16, 178176)
    OSTG = at("ostg", [128, 2048], F32, 163840)
    g8sb = at("g8sb", [8, D], F32, 188416)
    rowb = at("rowb", [1, D], F32, 192512)
    N2TS = at("n2ts", [128, 4096], BF16, 172032)
    SC = at("scbuf", [128, TC], F32, 180224)
    UH = at("uh", [128, 4096], BF16, 180224)
    wkb = at("wkb", [128, 2048], BF16, 188416)
    bcon_sb = at("bcon_sb", [2, D], F32, 188416)
    wvb = at("wvb", [128, 2048], BF16, 192512)
    ones2f = at("ones2f", [2, 128], F32, 192512)
    h1buf = at("h1buf", [128, 1024], BF16, 193536)
    wrb = at("wrb", [128, 2048], BF16, 196608)
    sqscr = at("sqscr", [128, 1024], BF16, 200704)
    xre = at("xre", [128, 1024], F32, 200704)
    rsl2 = at("rslot2", [128, 1024], F32, 204800)
    ST = {k: at("st_" + k, [128, NTB], F32, 208896 + j * 64) for j, k in
          enumerate(("sums", "sumsq", "mu", "ex2", "t1", "istd", "negmu"))}
    idf = at("idf", [128, 128], F32, 209344)
    idb = at("idb", [128, 128], BF16, 209856)
    dec1 = at("dec1", [128, 1], F32, 210112)
    sels = at("sels", [8, 1], F32, 210144)
    b1sb = at("b1sb", [128, 32], F32, 210176)
    b1tmp = at("b1tmp", [128, 32], F32, 210304)
    upb_sb = at("upb_sb", [128, 32], F32, 210432)
    lnw_sb = [at(f"lnw{i}s", [128, 8], F32, 210560 + i * 32) for i in range(2)]
    lnb_sb = [at(f"lnb{i}s", [128, 8], F32, 210624 + i * 32) for i in range(2)]
    ccols = at("ccols", [128, 8], F32, 210688)
    ccarry = at("ccarry", [128, 8], F32, 210720)
    pid_bf = at("pid_bf", [128, 128], BF16, 210752)
    pidT_f = at("pidT_f", [128, 128], F32, 211008)
    b1tmp2 = at("b1tmp2", [128, 32], F32, 211520)

    P = [nc.alloc_psum_tensor(f"p{j}", [128, 512], F32) for j in range(6)]
    ptrA = nc.alloc_psum_tensor("ptrA", [128, 1024], BF16)
    ptrB = nc.alloc_psum_tensor("ptrB", [128, 1024], BF16)

    ES = {"t": [], "v": [], "s": [], "g": [], "y": []}
    comp_names = ("pe", "v", "s", "cc", "ph")
    SEM = {n: nc.alloc_semaphore("s_" + n) for n in comp_names}
    cnt = {n: 0 for n in comp_names}

    def emit(eng, fn, inc=None, by=1):
        if eng == "v" and inc is None:
            inc = "v"
        if eng == "s" and inc is None:
            inc = "s"
        if eng in ("v", "s"):
            ES[eng].append(lambda e, s2=SEM[eng == "s" and "s" or "v"],
                           t=cnt["s" if eng == "s" else "v"]:
                           e.wait_ge(s2, t) if t else None)
        if inc is None:
            ES[eng].append(fn)
            return None
        cnt[inc] += by
        sem = SEM[inc]
        ES[eng].append(lambda e, f=fn, s2=sem, b=by: f(e).then_inc(s2, b))
        return cnt[inc]

    def wait(eng, sname, tgt):
        if tgt:
            ES[eng].append(lambda e, s2=SEM[sname], t=tgt: e.wait_ge(s2, t))

    class Q:

        def __init__(self, eng, name, depth=2):
            self.eng = eng
            self.sems = [nc.alloc_semaphore(f"q_{name}{j}") for j in range(depth)]
            self.tot = [0] * depth
            self.cur = 0
            self.opened = False

        def open(self):
            if not self.opened:
                j = self.cur
                if self.tot[j] > 0:
                    ES[self.eng].append(
                        lambda e, s2=self.sems[j], t=self.tot[j]: e.wait_ge(s2, t))
                self.opened = True
            return self.cur

        def dma(self, out, in_):
            j = self.open()
            self.tot[j] += 16
            ES[self.eng].append(
                lambda e, o=out, i=in_, s2=self.sems[j]:
                e.dma_start(out=o, in_=i).then_inc(s2, 16))
            return (j, self.tot[j])

        def close(self):
            j = self.cur
            tok = (j, self.tot[j])
            self.opened = False
            self.cur = (self.cur + 1) % len(self.sems)
            return tok

        def wait(self, eng, tok):
            if tok is None:
                return
            j, t = tok
            if t > 0:
                ES[eng].append(lambda e, s2=self.sems[j], t2=t: e.wait_ge(s2, t2))

        def wait_all(self, eng):
            for j in range(len(self.sems)):
                if self.tot[j] > 0:
                    ES[eng].append(
                        lambda e, s2=self.sems[j], t2=self.tot[j]: e.wait_ge(s2, t2))

    qy = Q("y", "ld")
    qw = Q("g", "w", 4)
    qo = Q("g", "o", 8)
    qo2 = Q("s", "o2", 8)
    qc = Q("g", "c", 1)
    qx = Q("g", "x")
    qr = Q("g", "r")
    qs = Q("g", "st")

    def barrier():
        cnt["ph"] += 5
        t = cnt["ph"]
        for k in ES:
            ES[k].append(lambda e, tt=t: (e.sem_inc(SEM["ph"], 1),
                                          e.wait_ge(SEM["ph"], tt)))

    def tap(name, buf, dt, cols, sem="v"):
        if name not in TAPS:
            return
        tp = nc.dram_tensor("tap_" + name, [buf.shape[0], cols], dt,
                            kind="ExternalOutput").ap()
        wait("g", sem, cnt[sem])
        qs.dma(tp, buf[:, 0:cols])
        tok = qs.close()
        for eng in ("t", "v", "s", "y"):
            qs.wait(eng, tok)

    qw.open()
    for tb in range(NTB):
        qw.dma(BIG1[:, tb * D:(tb + 1) * D], x_in[tb * 128:(tb + 1) * 128, :])
    g_x = qw.close()
    qy.dma(idf[:], ident)
    qy.dma(sels[:], sel)
    for i in range(2):
        qy.dma(lnw_sb[i][:], lnw[i])
        qy.dma(lnb_sb[i][:], lnb[i])
    qy.dma(upb_sb[:], upb_r)
    qy.dma(pidT_f[:], pidT)
    qy.dma(xre[:, 0:128], pident)
    g_p0 = qy.close()
    qy.wait("v", g_p0)
    emit("v", lambda e: e.tensor_copy(idb[:], idf[:]), "v")
    emit("v", lambda e: e.tensor_copy(pid_bf[:], xre[:, 0:128]), "v")
    emit("v", lambda e: e.memset(dec1[:], DECAY), "v")
    t_consts = cnt["v"]
    qy.wait("s", g_p0)
    qy.wait("t", g_p0)
    qy.wait("g", g_p0)
    qw.wait("v", g_x)
    qw.wait("s", g_x)
    qw.wait("t", g_x)
    qw.wait("g", g_x)

    def layer_norm(src_big, dst_big):
        wait("v", "s", cnt["s"])
        wait("s", "v", cnt["v"])
        wait("s", "pe", cnt["pe"])
        for tb in range(NTB):
            sl = src_big[:, tb * D:(tb + 1) * D]
            emit("v", lambda e, s2=sl, o=ST["sums"][:, tb:tb + 1]:
                 e.reduce_sum(o, s2, axis=AXX), "v")
            wait("s", "s", cnt["s"])
            emit("s", lambda e, s2=sl, o=ST["sumsq"][:, tb:tb + 1]:
                 e.activation(sqscr[:], s2, AF.Square, accum_out=o), "s")
        t_st_s = cnt["s"]
        wait("v", "s", t_st_s)
        emit("v", lambda e: e.tensor_scalar_mul(ST["mu"][:], ST["sums"][:], 1.0 / D))
        emit("v", lambda e: e.tensor_scalar_mul(ST["ex2"][:], ST["sumsq"][:], 1.0 / D))
        emit("v", lambda e: e.tensor_copy(ST["t1"][:], ST["mu"][:]))
        emit("v", lambda e: e.tensor_mul(ST["t1"][:], ST["t1"][:], ST["mu"][:]))
        emit("v", lambda e: e.tensor_sub(ST["ex2"][:], ST["ex2"][:], ST["t1"][:]))
        emit("v", lambda e: e.tensor_scalar_add(ST["ex2"][:], ST["ex2"][:], EPS),
             "v")
        t_varp = cnt["v"]
        wait("s", "v", t_varp)
        emit("s", lambda e: e.activation(ST["t1"][:], ST["ex2"][:], AF.Sqrt), "s")
        t_sq = cnt["s"]
        wait("v", "s", t_sq)
        emit("v", lambda e: e.reciprocal(ST["istd"][:], ST["t1"][:]))
        emit("v", lambda e: e.tensor_copy(ST["t1"][:], ST["istd"][:]))
        emit("v", lambda e: e.tensor_mul(ST["t1"][:], ST["t1"][:], ST["istd"][:]))
        emit("v", lambda e: e.tensor_mul(ST["t1"][:], ST["t1"][:], ST["ex2"][:]))
        emit("v", lambda e: e.tensor_scalar(ST["t1"][:], ST["t1"][:], -0.5, 1.5,
                                            ALU.mult, ALU.add))
        emit("v", lambda e: e.tensor_mul(ST["istd"][:], ST["istd"][:], ST["t1"][:]))
        emit("v", lambda e: e.tensor_mul(ST["negmu"][:], ST["mu"][:], ST["istd"][:]))
        emit("v", lambda e: e.tensor_scalar_mul(ST["negmu"][:], ST["negmu"][:],
                                                -1.0), "v")
        t_istd = cnt["v"]
        wait("s", "v", t_istd)
        t_aff = []
        for tb in range(NTB):
            sl = src_big[:, tb * D:(tb + 1) * D]
            dl = dst_big[:, tb * D:(tb + 1) * D]
            t_aff.append(emit("s", lambda e, s2=sl, d2=dl, tt=tb:
                              e.activation(d2, s2, AF.Identity,
                                           bias=ST["negmu"][:, tt:tt + 1],
                                           scale=ST["istd"][:, tt:tt + 1]), "s"))
        return t_aff

    def transpose_normed(t_aff, layer=None, src=None):
        if src is None:
            src = BIG4
        nb = 0
        for dc in range(DCN):
            for oct_ in range(2):
                pt = ptrA if nb % 2 == 0 else ptrB
                ev = "v" if nb % 2 == 0 else "s"
                nb += 1
                wait("t", "s", t_aff[oct_ * 8 + 7])
                wait("t", "v", cnt["v"])
                wait("t", "s", cnt["s"])
                for j in range(8):
                    tb = oct_ * 8 + j
                    emit("t", lambda e, o=pt[:, j * 128:(j + 1) * 128],
                         i2=src[:, tb * D + dc * 128: tb * D + (dc + 1) * 128]:
                         e.transpose(o, i2, idb[:]), "pe")
                wait(ev, "pe", cnt["pe"])
                dst = BIG3[:, dc * TC + oct_ * 1024: dc * TC + (oct_ + 1) * 1024]
                if layer is None:
                    if ev == "v":
                        emit("v", lambda e, d2=dst, p2=pt:
                             e.tensor_copy(d2, p2[:, 0:1024]), "v")
                    else:
                        emit("s", lambda e, d2=dst, p2=pt:
                             e.activation(d2, p2[:, 0:1024], AF.Identity), "s")
                else:
                    if ev == "v":
                        emit("v", lambda e, d2=dst, p2=pt, ii=layer, dd=dc:
                             e.tensor_scalar(d2, p2[:, 0:1024],
                                             lnw_sb[ii][:, dd:dd + 1],
                                             lnb_sb[ii][:, dd:dd + 1],
                                             ALU.mult, ALU.add), "v")
                    else:
                        emit("s", lambda e, d2=dst, p2=pt, ii=layer, dd=dc:
                             e.activation(d2, p2[:, 0:1024], AF.Identity,
                                          bias=lnb_sb[ii][:, dd:dd + 1],
                                          scale=lnw_sb[ii][:, dd:dd + 1]), "s")
        return (cnt["v"], cnt["s"])

    HSTG = [None, None]

    def chain_build(src, order, dst_col0, state, rhs0=None):
        nst = len(order)
        CH = [[BIG4[:, 8192 + bg * 2048 + h2 * 1024:
                    8192 + bg * 2048 + (h2 + 1) * 1024] for h2 in range(2)]
              for bg in range(4)]
        pairs = [(P[0], P[1]), (P[2], P[3]), (P[4], P[5])]
        t_evbg = [(cnt["v"], cnt["s"])] * 4
        t_pair = [(cnt["v"], cnt["s"])] * 3
        for fn in state["hwait"]:
            fn()
        state["hwait"] = []
        for s in range(nst):
            half = state["htog"]
            state["htog"] ^= 1
            fv, fs = state["fring"][half]
            wait("y", "v", fv)
            wait("y", "s", fs)
            qy.open()
            qy.dma(FST[half][:, 0:2048], src[order[s]][:, 0:2048])
            ty = qy.close()
            wait("s", "v", fv)
            qo2.open()
            qo2.dma(FST[half][:, 2048:4096], src[order[s]][:, 2048:4096])
            ts = qo2.close()
            qy.wait("v", ty)
            qo2.wait("s", ts)
            wait("v", "pe", state["hring"][half])
            wait("s", "pe", state["hring"][half])
            emit("v", lambda e, d=HSTG[half][:, 0:2048],
                 s3=FST[half][:, 0:2048]: e.tensor_copy(d, s3), "v")
            emit("s", lambda e, d=HSTG[half][:, 2048:4096],
                 s3=FST[half][:, 2048:4096]:
                 e.activation(d, s3, AF.Identity), "s")
            state["fring"][half] = (cnt["v"], cnt["s"])
            wait("t", "v", cnt["v"])
            wait("t", "s", cnt["s"])
            for bg in range(4):
                dslc = HSTG[half][:, bg * 1024:(bg + 1) * 1024]
                slot = bg % 3
                pa, pb = pairs[slot]
                wait("t", "v", t_pair[slot][0])
                wait("t", "s", t_pair[slot][1])
                if s > 0:
                    wait("t", "v", t_evbg[bg][0])
                    wait("t", "s", t_evbg[bg][1])
                for b8 in range(8):
                    rhs = (rhs0 if rhs0 is not None else idb[:]) if s == 0 else \
                        CH[bg][(s - 1) % 2][:, b8 * 128:(b8 + 1) * 128]
                    pj = pa if b8 < 4 else pb
                    emit("t", lambda e,
                         o=pj[:, (b8 % 4) * 128:(b8 % 4) * 128 + 128],
                         l2=dslc[:, b8 * 128:(b8 + 1) * 128], r=rhs:
                         e.matmul(o, l2, r, start=True, stop=True), "pe")
                wait("v", "pe", cnt["pe"])
                wait("s", "pe", cnt["pe"])
                if s < nst - 1:
                    d0, d1 = CH[bg][s % 2][:, 0:512], CH[bg][s % 2][:, 512:1024]
                else:
                    base = dst_col0 + bg * 1024
                    d0 = BIG3[:, base:base + 512]
                    d1 = BIG3[:, base + 512:base + 1024]
                emit("v", lambda e, d=d0, p2=pa: e.tensor_copy(d, p2[:]), "v")
                emit("s", lambda e, d=d1, p2=pb:
                     e.activation(d, p2[:], AF.Identity), "s")
                t_evbg[bg] = (cnt["v"], cnt["s"])
                t_pair[slot] = (cnt["v"], cnt["s"])
            state["hring"][half] = cnt["pe"]
        state["tdone"] = (cnt["v"], cnt["s"])

    wait("t", "v", t_consts)
    ch_state = {"hring": [0, 0], "htog": 0, "hwait": [], "tdone": (0, 0),
                "fring": [(0, 0), (0, 0)]}
    HSTG[0] = BIG4[:, 0:4096]
    HSTG[1] = BIG4[:, 4096:8192]
    chain_build(dsul, [6, 5, 4, 3, 2, 1, 0], 0, ch_state, rhs0=pid_bf[:])
    t_aprod, t_aprod_s = ch_state["tdone"]
    t_ln0 = layer_norm(BIG1, BIG1)

    wait("g", "v", t_aprod)
    qw.open()
    qw.dma(fstg[:, 0:32], bexp_c)
    g_bexp = qw.close()
    qw.wait("t", g_bexp)
    wait("t", "v", t_aprod)
    wait("t", "s", t_aprod_s)
    for c in range(32):
        emit("t", lambda e, cc2=c:
             e.matmul(P[4][:, cc2:cc2 + 1], BIG3[:, cc2 * 128:(cc2 + 1) * 128],
                      fstg[:, cc2:cc2 + 1], start=True, stop=True), "pe")
    wait("v", "pe", cnt["pe"])
    emit("v", lambda e: e.tensor_copy(b1tmp[:], P[4][:, 0:32]), "v")
    wait("t", "v", cnt["v"])
    emit("t", lambda e: e.matmul(P[5][:, 0:32], pidT_f[:], b1tmp[:],
                                 start=True, stop=True), "pe")
    wait("v", "pe", cnt["pe"])
    emit("v", lambda e: e.tensor_copy(b1tmp2[:], P[5][:, 0:32]), "v")
    wait("v", "v", cnt["v"])
    emit("v", lambda e: e.transpose(b1sb[:], b1tmp2[:]), "v")

    t_xr2 = [0, ch_state["hring"][1]]
    t_m1ev = [cnt["v"]] * 3
    tok_m1 = None
    BIG4v = BIG4[:, 0:4096].rearrange("p (l5 b56 cc2) -> p l5 b56 cc2",
                                      l5=32, b56=4, cc2=32)
    WXB = [dsb[:, 0:4096], BIG4[:, 4096:8192]]
    fring = ch_state["fring"]
    for dc in range(DCN):
        par = dc % 2
        fv, fs = fring[par]
        wait("y", "v", fv)
        wait("y", "s", fs)
        qy.open()
        qy.dma(FST[par][:, 0:2048], wexpT[dc][:, 0:2048])
        ty = qy.close()
        wait("s", "v", fv)
        qo2.open()
        qo2.dma(FST[par][:, 2048:4096], wexpT[dc][:, 2048:4096])
        ts = qo2.close()
        qy.wait("v", ty)
        qo2.wait("s", ts)
        wait("v", "pe", t_xr2[par])
        wait("s", "pe", t_xr2[par])
        emit("v", lambda e, d=WXB[par][:, 0:2048],
             s3=FST[par][:, 0:2048]: e.tensor_copy(d, s3), "v")
        emit("s", lambda e, d=WXB[par][:, 2048:4096],
             s3=FST[par][:, 2048:4096]:
             e.activation(d, s3, AF.Identity), "s")
        fring[par] = (cnt["v"], cnt["s"])
        wait("t", "v", cnt["v"])
        wait("t", "s", cnt["s"])
        for c4 in range(8):
            slot = c4 % 3
            pj = P[2 + slot]
            wait("t", "v", t_m1ev[slot])
            for q in range(4):
                c = c4 * 4 + q
                emit("t", lambda e, o=pj[:, q * 128:(q + 1) * 128],
                     l2=WXB[par][:, c * 128:(c + 1) * 128],
                     r2=BIG3[:, c * 128:(c + 1) * 128]:
                     e.matmul(o, l2, r2, start=True, stop=True), "pe")
            wait("v", "pe", cnt["pe"])
            qo.wait("v", tok_m1)
            c0 = c4 * 4
            srcv = pj[:].rearrange("p (cq l5 b56) -> p l5 b56 cq",
                                   cq=4, l5=32, b56=4)
            dstv = BIG4v[:, :, :, c0:c0 + 4]
            emit("v", lambda e, s2=srcv, d2=dstv: e.tensor_copy(d2, s2), "v")
            t_m1ev[slot] = cnt["v"]
        t_xr2[par] = cnt["pe"]
        wait("g", "v", cnt["v"])
        qo.open()
        qo.dma(m1p[dc], BIG4[:, 0:4096])
        tok_m1 = qo.close()

    ch_state["hring"] = [cnt["pe"], cnt["pe"]]
    ch_state["hwait"] = [lambda: qo.wait("v", tok_m1),
                         lambda: qo.wait("s", tok_m1)]

    chain_build(dsuh, [4, 3, 2, 1, 0], 4096, ch_state)
    wait("g", "v", cnt["v"])
    wait("s", "v", cnt["v"])
    qo2.open()
    qo2.dma(uhp, BIG3[:, 4096:8192])
    tok_uh = qo2.close()

    chain_build(dsdh, [0, 1, 2, 3, 4], 8192, ch_state)
    tok_f = None
    t_wcpar = list(ch_state["hring"])
    t_stage = [ch_state["tdone"], ch_state["tdone"]]
    tok_out = [None, None, None, None]
    WXB2 = [BIG4[:, 0:4096], BIG4[:, 4096:8192]]
    for gb in range(8):
        par = gb % 2
        fv, fs = fring[par]
        wait("y", "v", fv)
        wait("y", "s", fs)
        qy.open()
        qy.dma(FST[par][:, 0:2048], wconP[gb][:, 0:2048])
        ty = qy.close()
        wait("s", "v", fv)
        qo2.open()
        qo2.dma(FST[par][:, 2048:4096], wconP[gb][:, 2048:4096])
        ts = qo2.close()
        qy.wait("v", ty)
        qo2.wait("s", ts)
        wait("v", "pe", t_wcpar[par])
        wait("s", "pe", t_wcpar[par])
        if gb == 0:
            qo.wait("v", tok_m1)
            qo.wait("s", tok_m1)
        emit("v", lambda e, d=WXB2[par][:, 0:2048],
             s3=FST[par][:, 0:2048]: e.tensor_copy(d, s3), "v")
        emit("s", lambda e, d=WXB2[par][:, 2048:4096],
             s3=FST[par][:, 2048:4096]:
             e.activation(d, s3, AF.Identity), "s")
        fring[par] = (cnt["v"], cnt["s"])
        wait("t", "v", cnt["v"])
        wait("t", "s", cnt["s"])
        for q in range(4):
            g = gb * 4 + q
            slot = g % 2
            pa, pb = (P[2], P[3]) if slot == 0 else (P[4], P[5])
            wait("t", "v", t_stage[slot][0])
            wait("t", "s", t_stage[slot][1])
            lh = BIG3[:, 8192 + g * 128:8192 + (g + 1) * 128]
            emit("t", lambda e, l2=lh, w2=WXB2[par], b2=q * 1024, p2=pa:
                 e.matmul(p2[:], l2, w2[:, b2:b2 + 512],
                          start=True, stop=True), "pe")
            emit("t", lambda e, l2=lh, w2=WXB2[par], b2=q * 1024, p2=pb:
                 e.matmul(p2[:], l2, w2[:, b2 + 512:b2 + 1024],
                          start=True, stop=True), "pe")
            wait("v", "pe", cnt["pe"])
            wait("s", "pe", cnt["pe"])
            stg = dsb[:, (g % 4) * 1024:(g % 4 + 1) * 1024]
            if tok_out[g % 4] is not None:
                tq_, tk_ = tok_out[g % 4]
                tq_.wait("v", tk_)
                tq_.wait("s", tk_)
            emit("v", lambda e, d=stg[:, 0:512], p2=pa:
                 e.tensor_copy(d, p2[:]), "v")
            emit("s", lambda e, d=stg[:, 512:1024], p2=pb:
                 e.activation(d, p2[:], AF.Identity), "s")
            t_stage[slot] = (cnt["v"], cnt["s"])
            wait("g", "v", cnt["v"])
            wait("g", "s", cnt["s"])
            qo.open()
            qo.dma(m2d[g * 128:(g + 1) * 128, :], stg)
            tok_out[g % 4] = (qo, qo.close())
        t_wcpar[par] = cnt["pe"]
    ch_state["hring"] = [t_wcpar[0], t_wcpar[1]]

    chain_build(dsdl, [0, 1, 2, 3, 4, 5, 6], 12288, ch_state)
    m2r = m2d.rearrange("(l cc2) d -> cc2 l d", l=128, cc2=32)
    wcfr = wcf.rearrange("(l5 b56 cc2) d -> cc2 b56 l5 d", l5=32, b56=4, cc2=32)
    qo.wait_all("y")
    qo.wait_all("g")
    t_wcpar = list(ch_state["hring"])
    t_stage = [ch_state["tdone"], ch_state["tdone"]]
    for cb in range(8):
        par = cb % 2
        wait("y", "pe", t_wcpar[par])
        wait("g", "pe", t_wcpar[par])
        qy.open()
        qy.dma(WXB2[par][:, 0:2048], m2r[cb * 4:cb * 4 + 2]
               .rearrange("q p e -> p q e"))
        gtok = qy.close()
        qw.open()
        qw.dma(WXB2[par][:, 2048:4096], m2r[cb * 4 + 2:cb * 4 + 4]
               .rearrange("q p e -> p q e"))
        gtok2 = qw.close()
        qy.wait("t", gtok)
        qw.wait("t", gtok2)
        for q in range(4):
            c = cb * 4 + q
            slot = c % 2
            pa, pb = (P[2], P[3]) if slot == 0 else (P[4], P[5])
            wait("t", "v", t_stage[slot][0])
            wait("t", "s", t_stage[slot][1])
            lh = BIG3[:, 12288 + c * 128:12288 + (c + 1) * 128]
            emit("t", lambda e, l2=lh, w2=WXB2[par], b2=q * 1024, p2=pa:
                 e.matmul(p2[:], l2, w2[:, b2:b2 + 512],
                          start=True, stop=True), "pe")
            emit("t", lambda e, l2=lh, w2=WXB2[par], b2=q * 1024, p2=pb:
                 e.matmul(p2[:], l2, w2[:, b2 + 512:b2 + 1024],
                          start=True, stop=True), "pe")
            wait("v", "pe", cnt["pe"])
            wait("s", "pe", cnt["pe"])
            stg = dsb[:, (c % 4) * 1024:(c % 4 + 1) * 1024]
            if tok_out[c % 4] is not None:
                tq_, tk_ = tok_out[c % 4]
                tq_.wait("v", tk_)
                tq_.wait("s", tk_)
            emit("v", lambda e, d=stg[:, 0:512], p2=pa:
                 e.tensor_copy(d, p2[:]), "v")
            emit("s", lambda e, d=stg[:, 512:1024], p2=pb:
                 e.activation(d, p2[:], AF.Identity), "s")
            t_stage[slot] = (cnt["v"], cnt["s"])
            wait("s", "v", cnt["v"])
            qo2.open()
            qo2.dma(wcfr[c], stg)
            tok_f = qo2.close()
            tok_out[c % 4] = (qo2, tok_f)
        t_wcpar[par] = cnt["pe"]

    t_aff0 = layer_norm(BIG1, FSTN)

    barrier()

    t_wo_end = 0
    for i in range(2):
        if i == 0:
            t_aff = t_aff0
            src_n = FSTN
        else:
            t_aff = layer_norm(BIG1, BIG4)
            src_n = BIG4
        t_nt, t_nt_s = transpose_normed(t_aff, layer=i, src=src_n)
        if i == 0:
            tap("nt0", BIG3, BF16, 16384)

        tok_kv = [None] * ECN
        kvpar = [None] * ECN
        t_wpar = [0, 0]
        KS = [P[0], P[1], P[2], P[3]]
        VS = [P[4], P[5]]

        def ld_kv(ec, lay=i):
            wait("g", "pe", t_wpar[qw.cur % 2])
            par = qw.open() % 2
            hf2 = par * 1024
            qw.dma(wkb[:, hf2:hf2 + 1024], wk[lay][ec])
            qw.dma(wvb[:, hf2:hf2 + 1024], wv[lay][ec])
            kvpar[ec] = par
            tok_kv[ec] = qw.close()

        ld_kv(0)
        t_k05_last = 0
        t_vkv = 0
        for ec in range(ECN):
            if ec + 1 < ECN:
                ld_kv(ec + 1)
            qw.wait("t", tok_kv[ec])
            hf = kvpar[ec] * 1024
            wait("t", "v", t_nt)
            wait("t", "s", t_nt_s)
            wait("t", "s", t_k05_last)
            for dc in range(DCN):
                for t5 in range(NT5):
                    emit("t", lambda e, o=KS[t5][:],
                         l2=wkb[:, hf + dc * 128: hf + (dc + 1) * 128],
                         r2=BIG3[:, dc * TC + t5 * 512: dc * TC + (t5 + 1) * 512],
                         a=dc == 0, b=dc == DCN - 1:
                         e.matmul(o, l2, r2, start=a, stop=b), "pe")
            t_kpass = cnt["pe"]
            wait("s", "pe", t_kpass)
            wait("s", "v", cnt["v"])
            t_k05 = []
            for t5 in range(NT5):
                t_k05.append(emit("s", lambda e, o=SC[:, t5 * 512:(t5 + 1) * 512],
                                  p2=KS[t5]: e.activation(
                                      o, p2[:], AF.Identity, scale=1.0 - DECAY),
                                  "s"))
            t_k05_last = t_k05[-1]
            for pair in range(2):
                wait("t", "v", t_vkv)
                for dc in range(DCN):
                    for m in range(2):
                        t5 = pair * 2 + m
                        emit("t", lambda e, o=VS[m][:],
                             l2=wvb[:, hf + dc * 128: hf + (dc + 1) * 128],
                             r2=BIG3[:, dc * TC + t5 * 512: dc * TC + (t5 + 1) * 512],
                             a=dc == 0, b=dc == DCN - 1:
                             e.matmul(o, l2, r2, start=a, stop=b), "pe")
                t_vp = cnt["pe"]
                for m in range(2):
                    t5 = pair * 2 + m
                    wait("v", "pe", t_vp)
                    wait("v", "s", t_k05[t5])
                    dst = BIG2[:, ec * TC + t5 * 512: ec * TC + (t5 + 1) * 512]
                    emit("v", lambda e, d2=dst, sl2=SC[:, t5 * 512:(t5 + 1) * 512],
                         p2=VS[m]: e.tensor_mul(d2, sl2, p2[:]), "v")
                t_vkv = cnt["v"]
            t_wpar[kvpar[ec]] = cnt["pe"]
            srcv = BIG2[:, ec * TC:(ec + 1) * TC]
            wait("v", "v", cnt["v"])
            emit("v", lambda e, s2=srcv: e.tensor_tensor_scan(
                SC[:], _b0(dec1[:, 0:1], TC), s2, 0.0, ALU.mult, ALU.add), "v")
            emit("v", lambda e, cc2=ec: e.tensor_copy(
                ccols[:, cc2:cc2 + 1], SC[:, TC - 1:TC]), "v")
        t_contrib = cnt["v"]
        if i == 0:
            tap("kv0", BIG2, FP16, 16384)
            tap("cc0", ccols, F32, 8)

        tok_r = [None] * ECN
        rpar = [None] * ECN
        t_rpar = [0, 0]

        def ld_r(ec, lay=i):
            wait("g", "pe", t_rpar[qr.cur])
            par = qr.open()
            qr.dma(wrb[:, par * 1024:(par + 1) * 1024], wr[lay][ec])
            rpar[ec] = par
            tok_r[ec] = qr.close()

        ld_r(0)
        ld_r(1)
        qo.wait_all("g")
        qo2.wait_all("g")
        wait("g", "pe", t_wo_end)
        tok_wo = [None] * ECN

        def ld_wo(ec, lay=i):
            qw.open()
            qw.dma(wobA[:, ec * 512:(ec + 1) * 512], wo[lay][ec][:, 0:512])
            qw.dma(wobB[:, ec * 512:(ec + 1) * 512], wo[lay][ec][:, 512:1024])
            tok_wo[ec] = qw.close()

        ld_wo(0)
        SIG = [rsl2[:, 0:512], rsl2[:, 512:1024],
               xre[:, 0:512], xre[:, 512:1024]]
        t_y_ring = [0, 0, 0, 0]
        t_sigs = [[0] * NT5 for _ in range(ECN)]
        t_rp = [0] * ECN

        def emit_rpass(ec):
            hf = rpar[ec] * 1024
            qr.wait("t", tok_r[ec])
            if ec == 0:
                wait("t", "s", t_k05_last)
            else:
                wait("t", "s", t_sigs[ec - 1][NT5 - 1])
            for dc in range(DCN):
                for t5 in range(NT5):
                    emit("t", lambda e, o=KS[t5][:],
                         l2=wrb[:, hf + dc * 128: hf + (dc + 1) * 128],
                         r2=BIG3[:, dc * TC + t5 * 512: dc * TC + (t5 + 1) * 512],
                         a=dc == 0, b=dc == DCN - 1:
                         e.matmul(o, l2, r2, start=a, stop=b), "pe")
            t_rpar[rpar[ec]] = cnt["pe"]
            t_rp[ec] = cnt["pe"]

        def emit_sigs(ec):
            for t5 in range(NT5):
                wait("s", "pe", t_rp[ec])
                wait("s", "v", t_y_ring[t5])
                t_sigs[ec][t5] = emit("s", lambda e, o=SIG[t5], p2=KS[t5]:
                                      e.activation(o, p2[:], AF.Sigmoid), "s")

        qo.wait_all("v")
        qo.wait_all("g")
        wait("t", "v", t_contrib)
        for ec in range(ECN):
            emit("t", lambda e, o=P[4][0:1, ec % 4 * 128:(ec % 4 + 1) * 128]
                 if ec < 4 else P[5][0:1, (ec - 4) * 128:(ec - 3) * 128],
                 i2=ccols[:, ec:ec + 1]: e.transpose(o, i2, idf[:]), "pe")
        wait("v", "pe", cnt["pe"])
        emit("v", lambda e: e.tensor_copy(rowb[0:1, 0:512], P[4][0:1, :]), "v")
        t_ccrow = emit("v", lambda e: e.tensor_copy(rowb[0:1, 512:1024],
                                                    P[5][0:1, :]), "v")
        wait("g", "v", t_ccrow)
        qc.dma(cc_in[i], rowb[0:1, :])
        tok_ci = qc.close()
        qc.wait("g", tok_ci)
        t_cc = emit("g", lambda e, ii=i: e.collective_compute(
            "AllGather", ALU.bypass, replica_groups=[list(range(8))],
            ins=[cc_in[ii]], outs=[cc_out[ii]]), "cc", 1)
        wait("g", "cc", t_cc)
        wait("g", "pe", cnt["pe"])
        qc.dma(g8sb[:], cc_out[i])
        tok_g8 = qc.close()
        emit_rpass(0)
        emit_sigs(0)
        emit_rpass(1)

        qc.wait("t", tok_g8)
        wait("t", "v", t_ccrow)
        emit("t", lambda e: e.matmul(P[4][0:1, :], sels[:], g8sb[:, 0:512],
                                     start=True, stop=True), "pe")
        emit("t", lambda e: e.matmul(P[5][0:1, :], sels[:], g8sb[:, 512:1024],
                                     start=True, stop=True), "pe")
        wait("v", "pe", cnt["pe"])
        emit("v", lambda e: e.tensor_copy(rowb[0:1, 0:512], P[4][0:1, :]), "v")
        t_carow = emit("v", lambda e: e.tensor_copy(rowb[0:1, 512:1024],
                                                    P[5][0:1, :]), "v")
        wait("t", "v", t_carow)
        for ec in range(ECN):
            emit("t", lambda e, o=P[4][:, ec:ec + 1],
                 i2=rowb[0:1, ec * 128:(ec + 1) * 128]:
                 e.transpose(o, i2, idf[0:1, 0:1]), "pe")
        wait("v", "pe", cnt["pe"])
        t_carry = emit("v", lambda e: e.tensor_copy(ccarry[:], P[4][:, 0:8]),
                       "v")
        if i == 0:
            tap("ca0", ccarry, F32, 8)

        for ec in range(ECN):
            if ec >= 1:
                ld_wo(ec)
            srcv = BIG2[:, ec * TC:(ec + 1) * TC]
            wait("v", "v", cnt["v"])
            emit("v", lambda e, s2=srcv, cc2=ec: e.tensor_tensor_scan(
                SC[:], _b0(dec1[:, 0:1], TC), s2,
                ccarry[:, cc2:cc2 + 1], ALU.mult, ALU.add), "v")
            if ec >= 2:
                emit_rpass(ec)
            if ec + 2 < ECN:
                ld_r(ec + 2)
            if ec >= 1:
                emit_sigs(ec)
            for t5 in range(NT5):
                wait("v", "s", t_sigs[ec][t5])
                ydst = BIG4[:, ec * TC + t5 * 512: ec * TC + (t5 + 1) * 512]
                emit("v", lambda e, d2=ydst, sg=SIG[t5], t5_=t5: e.tensor_mul(
                    d2, sg, SC[:, t5_ * 512:(t5_ + 1) * 512]), "v")
                t_y_ring[t5] = cnt["v"]
        t_y_done = cnt["v"]
        if i == 0:
            tap("y0", BIG4, BF16, 16384)

        for ec in range(ECN):
            qw.wait("t", tok_wo[ec])
        wait("t", "v", t_y_done)
        XR = [xre, rsl2]
        tok_xr2 = [None, None]
        if i == 1:
            wait("g", "s", t_aff[-1])
            wait("g", "v", t_y_done)
            for pre in range(2):
                qx.open()
                qx.dma(XR[pre][:], x_in[pre * 128:(pre + 1) * 128, :])
                tok_xr2[pre] = qx.close()
        t_ev_ring = [0, 0]
        for tb in range(NTB):
            par = tb % 2
            pa, pb = (P[4], P[5]) if par == 0 else (P[0], P[1])
            wait("t", "v", t_ev_ring[par])
            for ec in range(ECN):
                emit("t", lambda e, o=pa[:],
                     l2=BIG4[:, ec * TC + tb * 128: ec * TC + (tb + 1) * 128],
                     r2=wobA[:, ec * 512:(ec + 1) * 512],
                     a=ec == 0, b=ec == ECN - 1:
                     e.matmul(o, l2, r2, start=a, stop=b), "pe")
            for ec in range(ECN):
                emit("t", lambda e, o=pb[:],
                     l2=BIG4[:, ec * TC + tb * 128: ec * TC + (tb + 1) * 128],
                     r2=wobB[:, ec * 512:(ec + 1) * 512],
                     a=ec == 0, b=ec == ECN - 1:
                     e.matmul(o, l2, r2, start=a, stop=b), "pe")
            t_mm = cnt["pe"]
            if i == 1:
                qx.wait("v", tok_xr2[tb % 2])
                sl = BIG1[:, tb * D:(tb + 1) * D]
                emit("v", lambda e, s2=sl, xr2=XR[tb % 2]:
                     e.scalar_tensor_tensor(
                         s2, xr2[:], 1.0, s2, ALU.mult, ALU.add), "v")
                if tb + 2 < NTB:
                    wait("g", "v", cnt["v"])
                    qx.open()
                    qx.dma(XR[tb % 2][:], x_in[(tb + 2) * 128:(tb + 3) * 128, :])
                    tok_xr2[tb % 2] = qx.close()
            wait("v", "pe", t_mm)
            sl0 = BIG1[:, tb * D: tb * D + 512]
            sl1 = BIG1[:, tb * D + 512: tb * D + 1024]
            emit("v", lambda e, s2=sl0, p2=pa: e.tensor_add(s2, s2, p2[:]), "v")
            emit("v", lambda e, s2=sl1, p2=pb: e.tensor_add(s2, s2, p2[:]), "v")
            t_ev_ring[par] = cnt["v"]
        t_wo_end = cnt["pe"]
        if i == 0:
            tap("s0", BIG1, F32, 16384)

    tap("x1", BIG1, F32, 16384)
    t_big1_final = cnt["v"]
    wcfh = wcf.rearrange("(h l5 p) d -> h p l5 d", h=2, l5=16, p=128)
    qo2.wait("g", tok_f)
    wait("g", "pe", t_wo_end)
    qw.open()
    qw.dma(WCONB[:, :].rearrange("p (l5 d) -> p l5 d", l5=16), wcfh[1])
    g_wcb = qw.close()
    wait("s", "v", t_big1_final)
    qo2.open()
    qo2.dma(x1d[:, 0:8192], BIG1[:, 0:8192])
    tok_x1d = qo2.close()
    wait("y", "v", t_big1_final)
    qy.open()
    qy.dma(x1d[:, 8192:16384], BIG1[:, 8192:16384])
    tok_x1d2 = qy.close()
    t_aff = layer_norm(BIG1, FSTN)
    transpose_normed(t_aff, layer=None, src=FSTN)
    wait("g", "v", cnt["v"])
    wait("g", "s", cnt["s"])
    qo.open()
    qo.dma(n2td[:, 0:8192], BIG3[:, 0:8192])
    tok_spill = qo.close()
    wait("y", "v", cnt["v"])
    wait("y", "s", cnt["s"])
    qy.open()
    qy.dma(n2td[:, 8192:16384], BIG3[:, 8192:16384])
    tok_sp2 = qy.close()
    barrier()

    qo.wait("y", tok_spill)
    qo.wait("v", tok_spill)
    qo.wait("g", tok_spill)
    qo.wait("s", tok_spill)
    qy.wait("y", tok_sp2)
    qy.wait("s", tok_sp2)
    qo2.wait("y", tok_x1d)
    qo2.wait("g", tok_x1d)
    qo2.wait("v", tok_f)
    qo.wait("y", tok_m1)
    qo.wait("g", tok_m1)
    qo2.wait("y", tok_uh)
    qo2.wait("y", tok_f)
    qo2.wait("s", tok_f)
    qy.wait("y", tok_x1d2)
    qy.wait("g", tok_x1d2)
    qy.open()
    for dc in range(4):
        qy.dma(WUP[:, dc * H:(dc + 1) * H], m1p[dc])
    qy.dma(UH[:], uhp)
    qy.dma(bcon_sb[:], bcon2)
    g_wup = qy.close()
    qw.open()
    for dc in range(4, DCN):
        qw.dma(WUP[:, dc * H:(dc + 1) * H], m1p[dc])
    g_wup2 = qw.close()
    qo2.open()
    qo2.dma(WCONA[:, :].rearrange("p (l5 d) -> p l5 d", l5=16), wcfh[0])
    g_wca = qo2.close()
    qy.wait("v", g_wup)
    emit("v", lambda e: e.memset(ones2f[:], 1.0), "v")
    t_ones = cnt["v"]
    qy.wait("t", g_wup)
    qw.wait("t", g_wup2)
    qo2.wait("t", g_wca)
    qw.wait("t", g_wcb)
    wait("t", "v", t_ones)

    XR2 = [xre, rsl2]
    tok_fx = [None, None]
    wait("g", "v", t_big1_final)
    wait("g", "s", t_aff[-1])
    for pre in range(2):
        qx.open()
        qx.dma(XR2[pre][:], x1d[:, pre * D:(pre + 1) * D])
        tok_fx[pre] = qx.close()

    st_toks = []
    t_cev = 0
    t_cev_ring = [0, 0]
    t_h1_ring = [0, 0]
    t_gelu_ring = [0, 0]
    t_exp_end = 0
    t_con_end = 0
    for t5 in range(NT5):
        if t5 >= 1:
            wait("y", "pe", t_exp_end)
        qy.open()
        for dc in range(DCN):
            qy.dma(N2TS[:, dc * 512:(dc + 1) * 512],
                   n2td[:, dc * TC + t5 * 512: dc * TC + (t5 + 1) * 512])
        tok_n2 = qy.close()
        qy.wait("t", tok_n2)
        t_gelu = []

        def emit_expand(l5):
            m = l5 % 2
            pex = P[0] if m == 0 else P[1]
            wait("t", "v", t_h1_ring[m])
            for dc in range(DCN):
                emit("t", lambda e, p2=pex,
                     l2=WUP[:, dc * H + l5 * 128: dc * H + (l5 + 1) * 128],
                     r2=N2TS[:, dc * 512:(dc + 1) * 512],
                     a=dc == 0, b=dc == DCN - 1:
                     e.matmul(p2[:], l2, r2, start=a, stop=b), "pe")
            return cnt["pe"]

        def emit_h1(l5, t_exp):
            m = l5 % 2
            pex = P[0] if m == 0 else P[1]
            wait("v", "pe", t_exp)
            t_h1 = emit("v", lambda e, p2=pex, h2=m * 512, l2=l5: e.tensor_add(
                h1buf[:, h2:h2 + 512], p2[:],
                _b0(b1sb[:, l2:l2 + 1], 512)), "v")
            t_h1_ring[m] = t_h1
            return t_h1

        def emit_ghigh_gelu(l5, t_h1, t5_):
            m = l5 % 2
            pgh = P[2] if m == 0 else P[3]
            wait("t", "v", t_h1)
            wait("t", "s", t_gelu_ring[m])
            wait("t", "v", t_cev_ring[1])
            emit("t", lambda e, l2=UH[:, l5 * 128:(l5 + 1) * 128],
                 r2=h1buf[:, m * 512:m * 512 + 512], p2=pgh:
                 e.matmul(p2[:], l2, r2, start=True, stop=True), "pe")
            t_gh = cnt["pe"]
            wait("s", "pe", t_gh)
            if t5_ >= 1:
                wait("s", "pe", t_con_end)
            t_gelu.append(emit("s", lambda e, p2=pgh, l2=l5: e.activation(
                GBUF[:, l2 * 512:(l2 + 1) * 512], p2[:], AF.Gelu,
                bias=upb_sb[:, l2:l2 + 1]), "s"))
            t_gelu_ring[m] = t_gelu[-1]

        th_prev = None
        for l5 in range(32):
            t_exp = emit_expand(l5)
            t_exp_end = t_exp
            if l5 >= 1:
                emit_ghigh_gelu(l5 - 1, th_prev, t5)
            th_prev = emit_h1(l5, t_exp)
        emit_ghigh_gelu(31, th_prev, t5)
        t_gelu_all = t_gelu[-1]

        for tq in range(4):
            tbg = t5 * 4 + tq
            pa, pb = (P[4], P[5]) if tq % 2 == 0 else (P[2], P[3])
            wait("t", "s", t_gelu_all)
            wait("t", "v", t_cev_ring[tq % 2])
            if tq % 2 == 1:
                wait("t", "s", t_gelu_all)
            emit("t", lambda e, p2=pa: e.matmul(p2[:], ones2f[:],
                                                bcon_sb[:, 0:512], start=True,
                                                stop=False), "pe")
            emit("t", lambda e, p2=pb: e.matmul(p2[:], ones2f[:],
                                                bcon_sb[:, 512:1024],
                                                start=True, stop=False), "pe")
            for l5 in range(32):
                lh = GBUF[:, l5 * 512 + tq * 128: l5 * 512 + (tq + 1) * 128]
                wt = WCONA if l5 < 16 else WCONB
                emit("t", lambda e, l2=lh, p2=pa,
                     r2=wt[:, (l5 % 16) * 1024:(l5 % 16) * 1024 + 512],
                     b=l5 == 31:
                     e.matmul(p2[:], l2, r2, start=False, stop=b), "pe")
                emit("t", lambda e, l2=lh, p2=pb,
                     r2=wt[:, (l5 % 16) * 1024 + 512:(l5 % 16) * 1024 + 1024],
                     b=l5 == 31:
                     e.matmul(p2[:], l2, r2, start=False, stop=b), "pe")
            t_con_end = cnt["pe"]
            qx.wait("v", tok_fx[tbg % 2])
            wait("v", "pe", t_con_end)
            off = (tbg % 2) * 1024
            if len(st_toks) >= 2:
                qs.wait("v", st_toks[-2])
            emit("v", lambda e, o2=off, xr=XR2[tbg % 2], p2=pa: e.tensor_add(
                OSTG[:, o2:o2 + 512], xr[:, 0:512], p2[:]), "v")
            t_ost = emit("v", lambda e, o2=off, xr=XR2[tbg % 2], p2=pb:
                         e.tensor_add(OSTG[:, o2 + 512:o2 + 1024],
                                      xr[:, 512:1024], p2[:]), "v")
            wait("g", "v", t_ost)
            qs.open()
            qs.dma(y_out[tbg * 128:(tbg + 1) * 128, :], OSTG[:, off:off + 1024])
            st_toks.append(qs.close())
            if tbg + 2 < 16:
                qx.open()
                qx.dma(XR2[tbg % 2][:], x1d[:, (tbg + 2) * D:(tbg + 3) * D])
                tok_fx[tbg % 2] = qx.close()
            t_cev_ring[tq % 2] = t_ost
            t_cev = t_ost

    qs.wait_all("g")

    with nc.Block() as block:
        @block.tensor
        def _(e):
            for f in ES["t"]:
                f(e)

        @block.vector
        def _(e):
            for f in ES["v"]:
                f(e)

        @block.scalar
        def _(e):
            for f in ES["s"]:
                f(e)

        @block.gpsimd
        def _(e):
            for f in ES["g"]:
                f(e)

        @block.sync
        def _(e):
            for f in ES["y"]:
                f(e)

    return nc


def _dense_low(w, s, transpose):
    out = np.zeros((32, 128, 128), np.float32)
    s_ = 1 << s
    i = np.arange(4096)
    i = i[(i & s_) == 0]
    k = ((i >> (s + 1)) << s) | (i & (s_ - 1))
    c = i >> 7
    r = i & 127
    r2 = r + s_
    if not transpose:
        out[c, r, r] = w[k, 0, 0]
        out[c, r, r2] = w[k, 0, 1]
        out[c, r2, r] = w[k, 1, 0]
        out[c, r2, r2] = w[k, 1, 1]
    else:
        out[c, r, r] = w[k, 0, 0]
        out[c, r2, r] = w[k, 0, 1]
        out[c, r, r2] = w[k, 1, 0]
        out[c, r2, r2] = w[k, 1, 1]
    return out


def _dense_high(w, s, transpose):
    out = np.zeros((32, 128, 128), np.float32)
    sb = 1 << (s - 7)
    cc = np.arange(32)
    cc = cc[(cc & sb) == 0]
    cc2 = cc + sb
    if not transpose:
        for l5 in range(32):
            for b56 in range(4):
                jA = cc * 128 + b56 * 32 + l5
                kA = ((jA >> (s + 1)) << s) | (jA & ((1 << s) - 1))
                out[l5, b56 * 32 + cc, b56 * 32 + cc] = w[kA, 0, 0]
                out[l5, b56 * 32 + cc, b56 * 32 + cc2] = w[kA, 0, 1]
                out[l5, b56 * 32 + cc2, b56 * 32 + cc] = w[kA, 1, 0]
                out[l5, b56 * 32 + cc2, b56 * 32 + cc2] = w[kA, 1, 1]
    else:
        for gl in range(32):
            for m in range(4):
                l = 4 * gl + m
                jA = cc * 128 + l
                kA = ((jA >> (s + 1)) << s) | (jA & ((1 << s) - 1))
                out[gl, m * 32 + cc, m * 32 + cc] = w[kA, 0, 0]
                out[gl, m * 32 + cc2, m * 32 + cc] = w[kA, 0, 1]
                out[gl, m * 32 + cc, m * 32 + cc2] = w[kA, 1, 0]
                out[gl, m * 32 + cc2, m * 32 + cc2] = w[kA, 1, 1]
    return out


def kernel(L0_attn_h, L0_attn_ln_w, L0_attn_ln_b, L0_attn_Wk, L0_attn_Wv,
           L0_attn_Wr, L0_attn_Wo, L0_ffn_W_exp, L0_ffn_b_exp,
           L0_ffn_up_weights, L0_ffn_up_bias, L0_ffn_down_weights,
           L0_ffn_W_con, L0_ffn_b_con, L0_ffn_down_bias):
    f32 = np.float32
    x = np.ascontiguousarray(np.asarray(L0_attn_h, f32))
    Wk = np.asarray(L0_attn_Wk, f32)
    Wv = np.asarray(L0_attn_Wv, f32)
    Wr = np.asarray(L0_attn_Wr, f32)
    Wo = np.asarray(L0_attn_Wo, f32)
    ln_w = np.asarray(L0_attn_ln_w, f32)
    ln_b = np.asarray(L0_attn_ln_b, f32)
    We = np.asarray(L0_ffn_W_exp, f32)
    b_exp = np.asarray(L0_ffn_b_exp, f32)
    upw = np.asarray(L0_ffn_up_weights, f32)
    upb = np.asarray(L0_ffn_up_bias, f32)
    dnw = np.asarray(L0_ffn_down_weights, f32)
    Wc = np.asarray(L0_ffn_W_con, f32)
    b_con = np.asarray(L0_ffn_b_con, f32)
    dnb = np.asarray(L0_ffn_down_bias, f32)

    def proj_tiles(W):
        WT = np.ascontiguousarray(W.T)
        return np.ascontiguousarray(
            WT.reshape(8, 128, 8, 128).transpose(2, 1, 0, 3).reshape(8, 128, D))

    def wo_tiles(W):
        return np.ascontiguousarray(W.T.reshape(8, 128, D))

    com = {
        "ident": np.eye(128, dtype=f32),
        "wexpT": np.ascontiguousarray(
            We.T.reshape(32, 128, 8, 128).transpose(2, 1, 0, 3)
            .reshape(8, 128, 4096)),
        "wconP": np.ascontiguousarray(
            Wc.reshape(32, 128, D).transpose(1, 0, 2).reshape(32, 128, D)
            .reshape(8, 4, 128, 1024).transpose(0, 2, 1, 3)
            .reshape(8, 128, 4096)),
        "dsul": np.ascontiguousarray(
            np.stack([_dense_low(upw[s], s, False) for s in range(7)])
            .transpose(0, 2, 1, 3).reshape(7, 128, 4096)),
        "dsuh": np.ascontiguousarray(
            np.stack([_dense_high(upw[s], s, False) for s in range(7, 12)])
            .transpose(0, 2, 1, 3).reshape(5, 128, 4096)),
        "dsdl": np.ascontiguousarray(
            np.stack([_dense_low(dnw[s], s, True) for s in range(7)])
            .transpose(0, 2, 1, 3).reshape(7, 128, 4096)),
        "dsdh": np.ascontiguousarray(
            np.stack([_dense_high(dnw[s], s, True) for s in range(7, 12)])
            .transpose(0, 2, 1, 3).reshape(5, 128, 4096)),
        "bexp_c": np.ascontiguousarray(b_exp.reshape(32, 128).T),
        "upb_r": np.ascontiguousarray(
            upb.reshape(32, 4, 32).transpose(1, 0, 2).reshape(128, 32)),
        "bcon2": np.ascontiguousarray(np.stack([b_con, dnb])),
    }
    pid = np.zeros((128, 128), f32)
    jn = np.arange(128)
    pid[(jn % 4) * 32 + jn // 4, jn] = 1.0
    com["pident"] = pid
    com["pidT"] = np.ascontiguousarray(pid.T)
    for i in range(2):
        com[f"wk{i}"] = proj_tiles(Wk[i])
        com[f"wv{i}"] = proj_tiles(Wv[i])
        com[f"wr{i}"] = proj_tiles(Wr[i])
        com[f"wo{i}"] = wo_tiles(Wo[i])
        com[f"lnw{i}"] = np.ascontiguousarray(ln_w[i].reshape(8, 128).T)
        com[f"lnb{i}"] = np.ascontiguousarray(ln_b[i].reshape(8, 128).T)

    nc = build_nc()
    in_maps = []
    for c in range(8):
        b, h = c // 2, c % 2
        selv = np.zeros((8, 1), f32)
        if h == 1:
            selv[c - 1, 0] = 1.0
        im = dict(com)
        im["x_in"] = np.ascontiguousarray(x[b, h * TC:(h + 1) * TC, :])
        im["sel"] = selv
        in_maps.append(im)

    res = run_bass_kernel_spmd(nc, in_maps, core_ids=list(range(8)))
    global LAST_RES
    LAST_RES = res.results
    out = np.empty((4, 4096, D), f32)
    for c in range(8):
        b, h = c // 2, c % 2
        out[b, h * TC:(h + 1) * TC, :] = res.results[c]["y_out"]
    return out


# version 21 (final)
# speedup vs baseline: 1.0584x; 1.0105x over previous
import dataclasses
import numpy as np

import concourse.bass as bass
import concourse.mybir as mybir
from concourse.bass_utils import run_bass_kernel_spmd

F32 = mybir.dt.float32
BF16 = mybir.dt.bfloat16
FP16 = mybir.dt.float16
AF = mybir.ActivationFunctionType
ALU = mybir.AluOpType
AXX = mybir.AxisListType.X

TAPS = ()
LAST_RES = None

D = 1024
TC = 2048
ECN = 8
DCN = 8
NT5 = 4
NTB = 16
H = 4096
DECAY = 0.95
EPS = 1e-5


def _b0(ap, n):
    return dataclasses.replace(ap, ap=[list(p) for p in ap.ap[:-1]] + [[0, n]])


def build_nc():
    nc = bass.Bass("TRN2", target_bir_lowering=False)

    x_in = nc.dram_tensor("x_in", [TC, D], F32, kind="ExternalInput").ap()
    ident = nc.dram_tensor("ident", [128, 128], F32, kind="ExternalInput").ap()
    sel = nc.dram_tensor("sel", [8, 1], F32, kind="ExternalInput").ap()
    wk = [nc.dram_tensor(f"wk{i}", [ECN, 128, D], F32, kind="ExternalInput").ap() for i in range(2)]
    wv = [nc.dram_tensor(f"wv{i}", [ECN, 128, D], F32, kind="ExternalInput").ap() for i in range(2)]
    wr = [nc.dram_tensor(f"wr{i}", [ECN, 128, D], F32, kind="ExternalInput").ap() for i in range(2)]
    wo = [nc.dram_tensor(f"wo{i}", [ECN, 128, D], F32, kind="ExternalInput").ap() for i in range(2)]
    lnw = [nc.dram_tensor(f"lnw{i}", [128, 8], F32, kind="ExternalInput").ap() for i in range(2)]
    lnb = [nc.dram_tensor(f"lnb{i}", [128, 8], F32, kind="ExternalInput").ap() for i in range(2)]
    wexpT = nc.dram_tensor("wexpT", [DCN, 128, 4096], F32, kind="ExternalInput").ap()
    wconP = nc.dram_tensor("wconP", [8, 128, 4096], F32, kind="ExternalInput").ap()
    dsul = nc.dram_tensor("dsul", [7, 128, 4096], F32, kind="ExternalInput").ap()
    dsuh = nc.dram_tensor("dsuh", [5, 128, 4096], F32, kind="ExternalInput").ap()
    dsdl = nc.dram_tensor("dsdl", [7, 128, 4096], F32, kind="ExternalInput").ap()
    dsdh = nc.dram_tensor("dsdh", [5, 128, 4096], F32, kind="ExternalInput").ap()
    bexp_c = nc.dram_tensor("bexp_c", [128, 32], F32, kind="ExternalInput").ap()
    pident = nc.dram_tensor("pident", [128, 128], F32, kind="ExternalInput").ap()
    pidT = nc.dram_tensor("pidT", [128, 128], F32, kind="ExternalInput").ap()
    upb_r = nc.dram_tensor("upb_r", [128, 32], F32, kind="ExternalInput").ap()
    bcon2 = nc.dram_tensor("bcon2", [2, D], F32, kind="ExternalInput").ap()
    y_out = nc.dram_tensor("y_out", [TC, D], F32, kind="ExternalOutput").ap()

    m1p = nc.dram_tensor("m1p", [DCN, 128, H], BF16).ap()
    uhp = nc.dram_tensor("uhp", [128, H], BF16).ap()
    m2d = nc.dram_tensor("m2d", [H, D], BF16).ap()
    wcf = nc.dram_tensor("wcf", [H, D], BF16).ap()
    n2td = nc.dram_tensor("n2td", [128, 16384], BF16).ap()
    x1d = nc.dram_tensor("x1d", [128, 16384], F32).ap()
    cc_in = [nc.dram_tensor(f"cc_in{i}", [1, D], F32).ap() for i in range(2)]
    cc_out = [nc.dram_tensor(f"cc_out{i}", [8, D], F32).ap() for i in range(2)]

    B0 = ((nc.sbuf_base + 31) // 32) * 32

    def at(name, shape, dt, off):
        return nc.alloc_sbuf_tensor_at(name, shape, dt, offset=B0 + off)

    BIG1 = at("big1", [128, 16384], F32, 0)
    WUP = at("wup", [128, 32768], BF16, 0)
    BIG2 = at("big2", [128, 16384], FP16, 65536)
    GBUF = at("gbuf", [128, 16384], BF16, 65536)
    FST = [at("fst0", [128, 4096], F32, 65536),
           at("fst1", [128, 4096], F32, 81920)]
    FSTN = at("fstn", [128, 16384], BF16, 65536)
    BIG3 = at("big3", [128, 16384], BF16, 98304)
    WCONA = at("wcona", [128, 16384], BF16, 98304)
    BIG4 = at("big4", [128, 16384], BF16, 131072)
    WCONB = at("wconb", [128, 16384], BF16, 131072)
    wobA = at("wobA", [128, 4096], BF16, 163840)
    dsb = at("dsbuf", [128, 4096], BF16, 163840)
    wobB = at("wobB", [128, 4096], BF16, 172032)
    wxs = at("wxslot", [128, 1024], BF16, 172032)
    wcs = at("wcslot", [128, 2048], BF16, 174080)
    fstg = at("fstg", [128, 1024], BF16, 178176)
    OSTG = at("ostg", [128, 2048], F32, 163840)
    g8sb = at("g8sb", [8, D], F32, 188416)
    rowb = at("rowb", [1, D], F32, 192512)
    N2TS = at("n2ts", [128, 4096], BF16, 172032)
    SC = at("scbuf", [128, TC], F32, 180224)
    UH = at("uh", [128, 4096], BF16, 180224)
    wkb = at("wkb", [128, 2048], BF16, 188416)
    bcon_sb = at("bcon_sb", [2, D], F32, 188416)
    wvb = at("wvb", [128, 2048], BF16, 192512)
    ones2f = at("ones2f", [2, 128], F32, 192512)
    h1buf = at("h1buf", [128, 1024], BF16, 193536)
    wrb = at("wrb", [128, 2048], BF16, 196608)
    sqscr = at("sqscr", [128, 1024], BF16, 200704)
    xre = at("xre", [128, 1024], F32, 200704)
    rsl2 = at("rslot2", [128, 1024], F32, 204800)
    ST = {k: at("st_" + k, [128, NTB], F32, 208896 + j * 64) for j, k in
          enumerate(("sums", "sumsq", "mu", "ex2", "t1", "istd", "negmu"))}
    idf = at("idf", [128, 128], F32, 209344)
    idb = at("idb", [128, 128], BF16, 209856)
    dec1 = at("dec1", [128, 1], F32, 210112)
    sels = at("sels", [8, 1], F32, 210144)
    b1sb = at("b1sb", [128, 32], F32, 210176)
    b1tmp = at("b1tmp", [128, 32], F32, 210304)
    upb_sb = at("upb_sb", [128, 32], F32, 210432)
    lnw_sb = [at(f"lnw{i}s", [128, 8], F32, 210560 + i * 32) for i in range(2)]
    lnb_sb = [at(f"lnb{i}s", [128, 8], F32, 210624 + i * 32) for i in range(2)]
    ccols = at("ccols", [128, 8], F32, 210688)
    ccarry = at("ccarry", [128, 8], F32, 210720)
    pid_bf = at("pid_bf", [128, 128], BF16, 210752)
    pidT_f = at("pidT_f", [128, 128], F32, 211008)
    b1tmp2 = at("b1tmp2", [128, 32], F32, 211520)

    P = [nc.alloc_psum_tensor(f"p{j}", [128, 512], F32) for j in range(6)]
    ptrA = nc.alloc_psum_tensor("ptrA", [128, 1024], BF16)
    ptrB = nc.alloc_psum_tensor("ptrB", [128, 1024], BF16)

    ES = {"t": [], "v": [], "s": [], "g": [], "y": []}
    comp_names = ("pe", "v", "s", "cc", "ph")
    SEM = {n: nc.alloc_semaphore("s_" + n) for n in comp_names}
    cnt = {n: 0 for n in comp_names}

    def emit(eng, fn, inc=None, by=1):
        if eng == "v" and inc is None:
            inc = "v"
        if eng == "s" and inc is None:
            inc = "s"
        if eng in ("v", "s"):
            ES[eng].append(lambda e, s2=SEM[eng == "s" and "s" or "v"],
                           t=cnt["s" if eng == "s" else "v"]:
                           e.wait_ge(s2, t) if t else None)
        if inc is None:
            ES[eng].append(fn)
            return None
        cnt[inc] += by
        sem = SEM[inc]
        ES[eng].append(lambda e, f=fn, s2=sem, b=by: f(e).then_inc(s2, b))
        return cnt[inc]

    def wait(eng, sname, tgt):
        if tgt:
            ES[eng].append(lambda e, s2=SEM[sname], t=tgt: e.wait_ge(s2, t))

    class Q:

        def __init__(self, eng, name, depth=2):
            self.eng = eng
            self.sems = [nc.alloc_semaphore(f"q_{name}{j}") for j in range(depth)]
            self.tot = [0] * depth
            self.cur = 0
            self.opened = False

        def open(self):
            if not self.opened:
                j = self.cur
                if self.tot[j] > 0:
                    ES[self.eng].append(
                        lambda e, s2=self.sems[j], t=self.tot[j]: e.wait_ge(s2, t))
                self.opened = True
            return self.cur

        def dma(self, out, in_):
            j = self.open()
            self.tot[j] += 16
            ES[self.eng].append(
                lambda e, o=out, i=in_, s2=self.sems[j]:
                e.dma_start(out=o, in_=i).then_inc(s2, 16))
            return (j, self.tot[j])

        def close(self):
            j = self.cur
            tok = (j, self.tot[j])
            self.opened = False
            self.cur = (self.cur + 1) % len(self.sems)
            return tok

        def wait(self, eng, tok):
            if tok is None:
                return
            j, t = tok
            if t > 0:
                ES[eng].append(lambda e, s2=self.sems[j], t2=t: e.wait_ge(s2, t2))

        def wait_all(self, eng):
            for j in range(len(self.sems)):
                if self.tot[j] > 0:
                    ES[eng].append(
                        lambda e, s2=self.sems[j], t2=self.tot[j]: e.wait_ge(s2, t2))

    qy = Q("y", "ld")
    qw = Q("g", "w", 4)
    qo = Q("g", "o", 8)
    qo2 = Q("s", "o2", 8)
    qc = Q("g", "c", 1)
    qx = Q("g", "x")
    qr = Q("g", "r")
    qs = Q("g", "st")

    def barrier():
        cnt["ph"] += 5
        t = cnt["ph"]
        for k in ES:
            ES[k].append(lambda e, tt=t: (e.sem_inc(SEM["ph"], 1),
                                          e.wait_ge(SEM["ph"], tt)))

    def tap(name, buf, dt, cols, sem="v"):
        if name not in TAPS:
            return
        tp = nc.dram_tensor("tap_" + name, [buf.shape[0], cols], dt,
                            kind="ExternalOutput").ap()
        wait("g", sem, cnt[sem])
        qs.dma(tp, buf[:, 0:cols])
        tok = qs.close()
        for eng in ("t", "v", "s", "y"):
            qs.wait(eng, tok)

    qw.open()
    for tb in range(NTB):
        qw.dma(BIG1[:, tb * D:(tb + 1) * D], x_in[tb * 128:(tb + 1) * 128, :])
    g_x = qw.close()
    qy.dma(idf[:], ident)
    qy.dma(sels[:], sel)
    for i in range(2):
        qy.dma(lnw_sb[i][:], lnw[i])
        qy.dma(lnb_sb[i][:], lnb[i])
    qy.dma(upb_sb[:], upb_r)
    qy.dma(pidT_f[:], pidT)
    qy.dma(xre[:, 0:128], pident)
    g_p0 = qy.close()
    qy.wait("v", g_p0)
    emit("v", lambda e: e.tensor_copy(idb[:], idf[:]), "v")
    emit("v", lambda e: e.tensor_copy(pid_bf[:], xre[:, 0:128]), "v")
    emit("v", lambda e: e.memset(dec1[:], DECAY), "v")
    t_consts = cnt["v"]
    qy.wait("s", g_p0)
    qy.wait("t", g_p0)
    qy.wait("g", g_p0)
    qw.wait("v", g_x)
    qw.wait("s", g_x)
    qw.wait("t", g_x)
    qw.wait("g", g_x)

    def layer_norm(src_big, dst_big):
        wait("v", "s", cnt["s"])
        wait("s", "v", cnt["v"])
        wait("s", "pe", cnt["pe"])
        for tb in range(NTB):
            sl = src_big[:, tb * D:(tb + 1) * D]
            emit("v", lambda e, s2=sl, o=ST["sums"][:, tb:tb + 1]:
                 e.reduce_sum(o, s2, axis=AXX), "v")
            wait("s", "s", cnt["s"])
            emit("s", lambda e, s2=sl, o=ST["sumsq"][:, tb:tb + 1]:
                 e.activation(sqscr[:], s2, AF.Square, accum_out=o), "s")
        return ln_stats_finish(src_big, dst_big)

    def ln_stats_finish(src_big, dst_big):
        t_st_s = cnt["s"]
        wait("v", "s", t_st_s)
        wait("s", "v", cnt["v"])
        emit("v", lambda e: e.tensor_scalar_mul(ST["mu"][:], ST["sums"][:], 1.0 / D))
        emit("v", lambda e: e.tensor_scalar_mul(ST["ex2"][:], ST["sumsq"][:], 1.0 / D))
        emit("v", lambda e: e.tensor_copy(ST["t1"][:], ST["mu"][:]))
        emit("v", lambda e: e.tensor_mul(ST["t1"][:], ST["t1"][:], ST["mu"][:]))
        emit("v", lambda e: e.tensor_sub(ST["ex2"][:], ST["ex2"][:], ST["t1"][:]))
        emit("v", lambda e: e.tensor_scalar_add(ST["ex2"][:], ST["ex2"][:], EPS),
             "v")
        t_varp = cnt["v"]
        wait("s", "v", t_varp)
        emit("s", lambda e: e.activation(ST["t1"][:], ST["ex2"][:], AF.Sqrt), "s")
        t_sq = cnt["s"]
        wait("v", "s", t_sq)
        emit("v", lambda e: e.reciprocal(ST["istd"][:], ST["t1"][:]))
        emit("v", lambda e: e.tensor_copy(ST["t1"][:], ST["istd"][:]))
        emit("v", lambda e: e.tensor_mul(ST["t1"][:], ST["t1"][:], ST["istd"][:]))
        emit("v", lambda e: e.tensor_mul(ST["t1"][:], ST["t1"][:], ST["ex2"][:]))
        emit("v", lambda e: e.tensor_scalar(ST["t1"][:], ST["t1"][:], -0.5, 1.5,
                                            ALU.mult, ALU.add))
        emit("v", lambda e: e.tensor_mul(ST["istd"][:], ST["istd"][:], ST["t1"][:]))
        emit("v", lambda e: e.tensor_mul(ST["negmu"][:], ST["mu"][:], ST["istd"][:]))
        emit("v", lambda e: e.tensor_scalar_mul(ST["negmu"][:], ST["negmu"][:],
                                                -1.0), "v")
        t_istd = cnt["v"]
        wait("s", "v", t_istd)
        t_aff = []
        for tb in range(NTB):
            sl = src_big[:, tb * D:(tb + 1) * D]
            dl = dst_big[:, tb * D:(tb + 1) * D]
            t_aff.append(emit("s", lambda e, s2=sl, d2=dl, tt=tb:
                              e.activation(d2, s2, AF.Identity,
                                           bias=ST["negmu"][:, tt:tt + 1],
                                           scale=ST["istd"][:, tt:tt + 1]), "s"))
        return t_aff

    def transpose_normed(t_aff, layer=None, src=None):
        if src is None:
            src = BIG4
        nb = 0
        for dc in range(DCN):
            for oct_ in range(2):
                pt = ptrA if nb % 2 == 0 else ptrB
                ev = "v" if nb % 2 == 0 else "s"
                nb += 1
                wait("t", "s", t_aff[oct_ * 8 + 7])
                wait("t", "v", cnt["v"])
                wait("t", "s", cnt["s"])
                for j in range(8):
                    tb = oct_ * 8 + j
                    emit("t", lambda e, o=pt[:, j * 128:(j + 1) * 128],
                         i2=src[:, tb * D + dc * 128: tb * D + (dc + 1) * 128]:
                         e.transpose(o, i2, idb[:]), "pe")
                wait(ev, "pe", cnt["pe"])
                dst = BIG3[:, dc * TC + oct_ * 1024: dc * TC + (oct_ + 1) * 1024]
                if layer is None:
                    if ev == "v":
                        emit("v", lambda e, d2=dst, p2=pt:
                             e.tensor_copy(d2, p2[:, 0:1024]), "v")
                    else:
                        emit("s", lambda e, d2=dst, p2=pt:
                             e.activation(d2, p2[:, 0:1024], AF.Identity), "s")
                else:
                    if ev == "v":
                        emit("v", lambda e, d2=dst, p2=pt, ii=layer, dd=dc:
                             e.tensor_scalar(d2, p2[:, 0:1024],
                                             lnw_sb[ii][:, dd:dd + 1],
                                             lnb_sb[ii][:, dd:dd + 1],
                                             ALU.mult, ALU.add), "v")
                    else:
                        emit("s", lambda e, d2=dst, p2=pt, ii=layer, dd=dc:
                             e.activation(d2, p2[:, 0:1024], AF.Identity,
                                          bias=lnb_sb[ii][:, dd:dd + 1],
                                          scale=lnw_sb[ii][:, dd:dd + 1]), "s")
        return (cnt["v"], cnt["s"])

    HSTG = [None, None]

    def chain_build(src, order, dst_col0, state, rhs0=None):
        nst = len(order)
        CH = [[BIG4[:, 8192 + bg * 2048 + h2 * 1024:
                    8192 + bg * 2048 + (h2 + 1) * 1024] for h2 in range(2)]
              for bg in range(4)]
        pairs = [(P[0], P[1]), (P[2], P[3]), (P[4], P[5])]
        t_evbg = [(cnt["v"], cnt["s"])] * 4
        t_pair = [(cnt["v"], cnt["s"])] * 3
        for fn in state["hwait"]:
            fn()
        state["hwait"] = []
        for s in range(nst):
            half = state["htog"]
            state["htog"] ^= 1
            fv, fs = state["fring"][half]
            wait("y", "v", fv)
            wait("y", "s", fs)
            qy.open()
            qy.dma(FST[half][:, 0:2048], src[order[s]][:, 0:2048])
            ty = qy.close()
            wait("s", "v", fv)
            qo2.open()
            qo2.dma(FST[half][:, 2048:4096], src[order[s]][:, 2048:4096])
            ts = qo2.close()
            qy.wait("v", ty)
            qo2.wait("s", ts)
            wait("v", "pe", state["hring"][half])
            wait("s", "pe", state["hring"][half])
            emit("v", lambda e, d=HSTG[half][:, 0:2048],
                 s3=FST[half][:, 0:2048]: e.tensor_copy(d, s3), "v")
            emit("s", lambda e, d=HSTG[half][:, 2048:4096],
                 s3=FST[half][:, 2048:4096]:
                 e.activation(d, s3, AF.Identity), "s")
            state["fring"][half] = (cnt["v"], cnt["s"])
            wait("t", "v", cnt["v"])
            wait("t", "s", cnt["s"])
            for bg in range(4):
                dslc = HSTG[half][:, bg * 1024:(bg + 1) * 1024]
                slot = bg % 3
                pa, pb = pairs[slot]
                wait("t", "v", t_pair[slot][0])
                wait("t", "s", t_pair[slot][1])
                if s > 0:
                    wait("t", "v", t_evbg[bg][0])
                    wait("t", "s", t_evbg[bg][1])
                for b8 in range(8):
                    rhs = (rhs0 if rhs0 is not None else idb[:]) if s == 0 else \
                        CH[bg][(s - 1) % 2][:, b8 * 128:(b8 + 1) * 128]
                    pj = pa if b8 < 4 else pb
                    emit("t", lambda e,
                         o=pj[:, (b8 % 4) * 128:(b8 % 4) * 128 + 128],
                         l2=dslc[:, b8 * 128:(b8 + 1) * 128], r=rhs:
                         e.matmul(o, l2, r, start=True, stop=True), "pe")
                wait("v", "pe", cnt["pe"])
                wait("s", "pe", cnt["pe"])
                if s < nst - 1:
                    d0, d1 = CH[bg][s % 2][:, 0:512], CH[bg][s % 2][:, 512:1024]
                else:
                    base = dst_col0 + bg * 1024
                    d0 = BIG3[:, base:base + 512]
                    d1 = BIG3[:, base + 512:base + 1024]
                emit("v", lambda e, d=d0, p2=pa: e.tensor_copy(d, p2[:]), "v")
                emit("s", lambda e, d=d1, p2=pb:
                     e.activation(d, p2[:], AF.Identity), "s")
                t_evbg[bg] = (cnt["v"], cnt["s"])
                t_pair[slot] = (cnt["v"], cnt["s"])
            state["hring"][half] = cnt["pe"]
        state["tdone"] = (cnt["v"], cnt["s"])

    wait("t", "v", t_consts)
    ch_state = {"hring": [0, 0], "htog": 0, "hwait": [], "tdone": (0, 0),
                "fring": [(0, 0), (0, 0)]}
    HSTG[0] = BIG4[:, 0:4096]
    HSTG[1] = BIG4[:, 4096:8192]
    chain_build(dsul, [6, 5, 4, 3, 2, 1, 0], 0, ch_state, rhs0=pid_bf[:])
    t_aprod, t_aprod_s = ch_state["tdone"]
    t_ln0 = layer_norm(BIG1, BIG1)

    wait("g", "v", t_aprod)
    qw.open()
    qw.dma(fstg[:, 0:32], bexp_c)
    g_bexp = qw.close()
    qw.wait("t", g_bexp)
    wait("t", "v", t_aprod)
    wait("t", "s", t_aprod_s)
    for c in range(32):
        emit("t", lambda e, cc2=c:
             e.matmul(P[4][:, cc2:cc2 + 1], BIG3[:, cc2 * 128:(cc2 + 1) * 128],
                      fstg[:, cc2:cc2 + 1], start=True, stop=True), "pe")
    wait("v", "pe", cnt["pe"])
    emit("v", lambda e: e.tensor_copy(b1tmp[:], P[4][:, 0:32]), "v")
    wait("t", "v", cnt["v"])
    emit("t", lambda e: e.matmul(P[5][:, 0:32], pidT_f[:], b1tmp[:],
                                 start=True, stop=True), "pe")
    wait("v", "pe", cnt["pe"])
    emit("v", lambda e: e.tensor_copy(b1tmp2[:], P[5][:, 0:32]), "v")
    wait("v", "v", cnt["v"])
    emit("v", lambda e: e.transpose(b1sb[:], b1tmp2[:]), "v")

    t_xr2 = [0, ch_state["hring"][1]]
    t_m1ev = [cnt["v"]] * 3
    tok_m1 = None
    BIG4v = BIG4[:, 0:4096].rearrange("p (l5 b56 cc2) -> p l5 b56 cc2",
                                      l5=32, b56=4, cc2=32)
    WXB = [dsb[:, 0:4096], BIG4[:, 4096:8192]]
    fring = ch_state["fring"]
    for dc in range(DCN):
        par = dc % 2
        fv, fs = fring[par]
        wait("y", "v", fv)
        wait("y", "s", fs)
        qy.open()
        qy.dma(FST[par][:, 0:2048], wexpT[dc][:, 0:2048])
        ty = qy.close()
        wait("s", "v", fv)
        qo2.open()
        qo2.dma(FST[par][:, 2048:4096], wexpT[dc][:, 2048:4096])
        ts = qo2.close()
        qy.wait("v", ty)
        qo2.wait("s", ts)
        wait("v", "pe", t_xr2[par])
        wait("s", "pe", t_xr2[par])
        emit("v", lambda e, d=WXB[par][:, 0:2048],
             s3=FST[par][:, 0:2048]: e.tensor_copy(d, s3), "v")
        emit("s", lambda e, d=WXB[par][:, 2048:4096],
             s3=FST[par][:, 2048:4096]:
             e.activation(d, s3, AF.Identity), "s")
        fring[par] = (cnt["v"], cnt["s"])
        wait("t", "v", cnt["v"])
        wait("t", "s", cnt["s"])
        for c4 in range(8):
            slot = c4 % 3
            pj = P[2 + slot]
            wait("t", "v", t_m1ev[slot])
            for q in range(4):
                c = c4 * 4 + q
                emit("t", lambda e, o=pj[:, q * 128:(q + 1) * 128],
                     l2=WXB[par][:, c * 128:(c + 1) * 128],
                     r2=BIG3[:, c * 128:(c + 1) * 128]:
                     e.matmul(o, l2, r2, start=True, stop=True), "pe")
            wait("v", "pe", cnt["pe"])
            qo.wait("v", tok_m1)
            c0 = c4 * 4
            srcv = pj[:].rearrange("p (cq l5 b56) -> p l5 b56 cq",
                                   cq=4, l5=32, b56=4)
            dstv = BIG4v[:, :, :, c0:c0 + 4]
            emit("v", lambda e, s2=srcv, d2=dstv: e.tensor_copy(d2, s2), "v")
            t_m1ev[slot] = cnt["v"]
        t_xr2[par] = cnt["pe"]
        wait("g", "v", cnt["v"])
        qo.open()
        qo.dma(m1p[dc], BIG4[:, 0:4096])
        tok_m1 = qo.close()

    ch_state["hring"] = [cnt["pe"], cnt["pe"]]
    ch_state["hwait"] = [lambda: qo.wait("v", tok_m1),
                         lambda: qo.wait("s", tok_m1)]

    chain_build(dsuh, [4, 3, 2, 1, 0], 4096, ch_state)
    wait("g", "v", cnt["v"])
    wait("s", "v", cnt["v"])
    qo2.open()
    qo2.dma(uhp, BIG3[:, 4096:8192])
    tok_uh = qo2.close()

    chain_build(dsdh, [0, 1, 2, 3, 4], 8192, ch_state)
    tok_f = None
    t_wcpar = list(ch_state["hring"])
    t_stage = [ch_state["tdone"], ch_state["tdone"]]
    tok_out = [None, None, None, None]
    WXB2 = [BIG4[:, 0:4096], BIG4[:, 4096:8192]]
    for gb in range(8):
        par = gb % 2
        fv, fs = fring[par]
        wait("y", "v", fv)
        wait("y", "s", fs)
        qy.open()
        qy.dma(FST[par][:, 0:2048], wconP[gb][:, 0:2048])
        ty = qy.close()
        wait("s", "v", fv)
        qo2.open()
        qo2.dma(FST[par][:, 2048:4096], wconP[gb][:, 2048:4096])
        ts = qo2.close()
        qy.wait("v", ty)
        qo2.wait("s", ts)
        wait("v", "pe", t_wcpar[par])
        wait("s", "pe", t_wcpar[par])
        if gb == 0:
            qo.wait("v", tok_m1)
            qo.wait("s", tok_m1)
        emit("v", lambda e, d=WXB2[par][:, 0:2048],
             s3=FST[par][:, 0:2048]: e.tensor_copy(d, s3), "v")
        emit("s", lambda e, d=WXB2[par][:, 2048:4096],
             s3=FST[par][:, 2048:4096]:
             e.activation(d, s3, AF.Identity), "s")
        fring[par] = (cnt["v"], cnt["s"])
        wait("t", "v", cnt["v"])
        wait("t", "s", cnt["s"])
        for q in range(4):
            g = gb * 4 + q
            slot = g % 2
            pa, pb = (P[2], P[3]) if slot == 0 else (P[4], P[5])
            wait("t", "v", t_stage[slot][0])
            wait("t", "s", t_stage[slot][1])
            lh = BIG3[:, 8192 + g * 128:8192 + (g + 1) * 128]
            emit("t", lambda e, l2=lh, w2=WXB2[par], b2=q * 1024, p2=pa:
                 e.matmul(p2[:], l2, w2[:, b2:b2 + 512],
                          start=True, stop=True), "pe")
            emit("t", lambda e, l2=lh, w2=WXB2[par], b2=q * 1024, p2=pb:
                 e.matmul(p2[:], l2, w2[:, b2 + 512:b2 + 1024],
                          start=True, stop=True), "pe")
            wait("v", "pe", cnt["pe"])
            wait("s", "pe", cnt["pe"])
            stg = dsb[:, (g % 4) * 1024:(g % 4 + 1) * 1024]
            if tok_out[g % 4] is not None:
                tq_, tk_ = tok_out[g % 4]
                tq_.wait("v", tk_)
                tq_.wait("s", tk_)
            emit("v", lambda e, d=stg[:, 0:512], p2=pa:
                 e.tensor_copy(d, p2[:]), "v")
            emit("s", lambda e, d=stg[:, 512:1024], p2=pb:
                 e.activation(d, p2[:], AF.Identity), "s")
            t_stage[slot] = (cnt["v"], cnt["s"])
            wait("g", "v", cnt["v"])
            wait("g", "s", cnt["s"])
            qo.open()
            qo.dma(m2d[g * 128:(g + 1) * 128, :], stg)
            tok_out[g % 4] = (qo, qo.close())
        t_wcpar[par] = cnt["pe"]
    ch_state["hring"] = [t_wcpar[0], t_wcpar[1]]

    chain_build(dsdl, [0, 1, 2, 3, 4, 5, 6], 12288, ch_state)
    m2r = m2d.rearrange("(l cc2) d -> cc2 l d", l=128, cc2=32)
    wcfr = wcf.rearrange("(l5 b56 cc2) d -> cc2 b56 l5 d", l5=32, b56=4, cc2=32)
    qo.wait_all("y")
    qo.wait_all("g")
    t_wcpar = list(ch_state["hring"])
    t_stage = [ch_state["tdone"], ch_state["tdone"]]
    for cb in range(8):
        par = cb % 2
        wait("y", "pe", t_wcpar[par])
        wait("g", "pe", t_wcpar[par])
        qy.open()
        qy.dma(WXB2[par][:, 0:2048], m2r[cb * 4:cb * 4 + 2]
               .rearrange("q p e -> p q e"))
        gtok = qy.close()
        qw.open()
        qw.dma(WXB2[par][:, 2048:4096], m2r[cb * 4 + 2:cb * 4 + 4]
               .rearrange("q p e -> p q e"))
        gtok2 = qw.close()
        qy.wait("t", gtok)
        qw.wait("t", gtok2)
        for q in range(4):
            c = cb * 4 + q
            slot = c % 2
            pa, pb = (P[2], P[3]) if slot == 0 else (P[4], P[5])
            wait("t", "v", t_stage[slot][0])
            wait("t", "s", t_stage[slot][1])
            lh = BIG3[:, 12288 + c * 128:12288 + (c + 1) * 128]
            emit("t", lambda e, l2=lh, w2=WXB2[par], b2=q * 1024, p2=pa:
                 e.matmul(p2[:], l2, w2[:, b2:b2 + 512],
                          start=True, stop=True), "pe")
            emit("t", lambda e, l2=lh, w2=WXB2[par], b2=q * 1024, p2=pb:
                 e.matmul(p2[:], l2, w2[:, b2 + 512:b2 + 1024],
                          start=True, stop=True), "pe")
            wait("v", "pe", cnt["pe"])
            wait("s", "pe", cnt["pe"])
            stg = dsb[:, (c % 4) * 1024:(c % 4 + 1) * 1024]
            if tok_out[c % 4] is not None:
                tq_, tk_ = tok_out[c % 4]
                tq_.wait("v", tk_)
                tq_.wait("s", tk_)
            emit("v", lambda e, d=stg[:, 0:512], p2=pa:
                 e.tensor_copy(d, p2[:]), "v")
            emit("s", lambda e, d=stg[:, 512:1024], p2=pb:
                 e.activation(d, p2[:], AF.Identity), "s")
            t_stage[slot] = (cnt["v"], cnt["s"])
            wait("s", "v", cnt["v"])
            qo2.open()
            qo2.dma(wcfr[c], stg)
            tok_f = qo2.close()
            tok_out[c % 4] = (qo2, tok_f)
        t_wcpar[par] = cnt["pe"]

    t_aff0 = layer_norm(BIG1, FSTN)

    barrier()

    t_wo_end = 0
    for i in range(2):
        if i == 0:
            t_aff = t_aff0
            src_n = FSTN
        else:
            t_aff = layer_norm(BIG1, BIG4)
            src_n = BIG4
        t_nt, t_nt_s = transpose_normed(t_aff, layer=i, src=src_n)
        if i == 0:
            tap("nt0", BIG3, BF16, 16384)

        tok_kv = [None] * ECN
        kvpar = [None] * ECN
        t_wpar = [0, 0]
        KS = [P[0], P[1], P[2], P[3]]
        VS = [P[4], P[5]]

        def ld_kv(ec, lay=i):
            wait("g", "pe", t_wpar[qw.cur % 2])
            par = qw.open() % 2
            hf2 = par * 1024
            qw.dma(wkb[:, hf2:hf2 + 1024], wk[lay][ec])
            qw.dma(wvb[:, hf2:hf2 + 1024], wv[lay][ec])
            kvpar[ec] = par
            tok_kv[ec] = qw.close()

        ld_kv(0)
        t_k05_last = 0
        t_vkv = 0
        for ec in range(ECN):
            if ec + 1 < ECN:
                ld_kv(ec + 1)
            qw.wait("t", tok_kv[ec])
            hf = kvpar[ec] * 1024
            wait("t", "v", t_nt)
            wait("t", "s", t_nt_s)
            wait("t", "s", t_k05_last)
            for dc in range(DCN):
                for t5 in range(NT5):
                    emit("t", lambda e, o=KS[t5][:],
                         l2=wkb[:, hf + dc * 128: hf + (dc + 1) * 128],
                         r2=BIG3[:, dc * TC + t5 * 512: dc * TC + (t5 + 1) * 512],
                         a=dc == 0, b=dc == DCN - 1:
                         e.matmul(o, l2, r2, start=a, stop=b), "pe")
            t_kpass = cnt["pe"]
            wait("s", "pe", t_kpass)
            wait("s", "v", cnt["v"])
            t_k05 = []
            for t5 in range(NT5):
                t_k05.append(emit("s", lambda e, o=SC[:, t5 * 512:(t5 + 1) * 512],
                                  p2=KS[t5]: e.activation(
                                      o, p2[:], AF.Identity, scale=1.0 - DECAY),
                                  "s"))
            t_k05_last = t_k05[-1]
            for pair in range(2):
                wait("t", "v", t_vkv)
                for dc in range(DCN):
                    for m in range(2):
                        t5 = pair * 2 + m
                        emit("t", lambda e, o=VS[m][:],
                             l2=wvb[:, hf + dc * 128: hf + (dc + 1) * 128],
                             r2=BIG3[:, dc * TC + t5 * 512: dc * TC + (t5 + 1) * 512],
                             a=dc == 0, b=dc == DCN - 1:
                             e.matmul(o, l2, r2, start=a, stop=b), "pe")
                t_vp = cnt["pe"]
                for m in range(2):
                    t5 = pair * 2 + m
                    wait("v", "pe", t_vp)
                    wait("v", "s", t_k05[t5])
                    dst = BIG2[:, ec * TC + t5 * 512: ec * TC + (t5 + 1) * 512]
                    emit("v", lambda e, d2=dst, sl2=SC[:, t5 * 512:(t5 + 1) * 512],
                         p2=VS[m]: e.tensor_mul(d2, sl2, p2[:]), "v")
                t_vkv = cnt["v"]
            t_wpar[kvpar[ec]] = cnt["pe"]
            srcv = BIG2[:, ec * TC:(ec + 1) * TC]
            wait("v", "v", cnt["v"])
            emit("v", lambda e, s2=srcv: e.tensor_tensor_scan(
                SC[:], _b0(dec1[:, 0:1], TC), s2, 0.0, ALU.mult, ALU.add), "v")
            emit("v", lambda e, cc2=ec: e.tensor_copy(
                ccols[:, cc2:cc2 + 1], SC[:, TC - 1:TC]), "v")
        t_contrib = cnt["v"]
        if i == 0:
            tap("kv0", BIG2, FP16, 16384)
            tap("cc0", ccols, F32, 8)

        tok_r = [None] * ECN
        rpar = [None] * ECN
        t_rpar = [0, 0]

        def ld_r(ec, lay=i):
            wait("g", "pe", t_rpar[qr.cur])
            par = qr.open()
            qr.dma(wrb[:, par * 1024:(par + 1) * 1024], wr[lay][ec])
            rpar[ec] = par
            tok_r[ec] = qr.close()

        ld_r(0)
        ld_r(1)
        qo.wait_all("g")
        qo2.wait_all("g")
        wait("g", "pe", t_wo_end)
        tok_wo = [None] * ECN

        def ld_wo(ec, lay=i):
            qw.open()
            qw.dma(wobA[:, ec * 512:(ec + 1) * 512], wo[lay][ec][:, 0:512])
            qw.dma(wobB[:, ec * 512:(ec + 1) * 512], wo[lay][ec][:, 512:1024])
            tok_wo[ec] = qw.close()

        ld_wo(0)
        SIG = [rsl2[:, 0:512], rsl2[:, 512:1024],
               xre[:, 0:512], xre[:, 512:1024]]
        t_y_ring = [0, 0, 0, 0]
        t_sigs = [[0] * NT5 for _ in range(ECN)]
        t_rp = [0] * ECN

        def emit_rpass(ec):
            hf = rpar[ec] * 1024
            qr.wait("t", tok_r[ec])
            if ec == 0:
                wait("t", "s", t_k05_last)
            else:
                wait("t", "s", t_sigs[ec - 1][NT5 - 1])
            for dc in range(DCN):
                for t5 in range(NT5):
                    emit("t", lambda e, o=KS[t5][:],
                         l2=wrb[:, hf + dc * 128: hf + (dc + 1) * 128],
                         r2=BIG3[:, dc * TC + t5 * 512: dc * TC + (t5 + 1) * 512],
                         a=dc == 0, b=dc == DCN - 1:
                         e.matmul(o, l2, r2, start=a, stop=b), "pe")
            t_rpar[rpar[ec]] = cnt["pe"]
            t_rp[ec] = cnt["pe"]

        def emit_sigs(ec):
            for t5 in range(NT5):
                wait("s", "pe", t_rp[ec])
                wait("s", "v", t_y_ring[t5])
                t_sigs[ec][t5] = emit("s", lambda e, o=SIG[t5], p2=KS[t5]:
                                      e.activation(o, p2[:], AF.Sigmoid), "s")

        qo.wait_all("v")
        qo.wait_all("g")
        wait("t", "v", t_contrib)
        for ec in range(ECN):
            emit("t", lambda e, o=P[4][0:1, ec % 4 * 128:(ec % 4 + 1) * 128]
                 if ec < 4 else P[5][0:1, (ec - 4) * 128:(ec - 3) * 128],
                 i2=ccols[:, ec:ec + 1]: e.transpose(o, i2, idf[:]), "pe")
        wait("v", "pe", cnt["pe"])
        emit("v", lambda e: e.tensor_copy(rowb[0:1, 0:512], P[4][0:1, :]), "v")
        t_ccrow = emit("v", lambda e: e.tensor_copy(rowb[0:1, 512:1024],
                                                    P[5][0:1, :]), "v")
        wait("g", "v", t_ccrow)
        qc.dma(cc_in[i], rowb[0:1, :])
        tok_ci = qc.close()
        qc.wait("g", tok_ci)
        t_cc = emit("g", lambda e, ii=i: e.collective_compute(
            "AllGather", ALU.bypass, replica_groups=[list(range(8))],
            ins=[cc_in[ii]], outs=[cc_out[ii]]), "cc", 1)
        wait("g", "cc", t_cc)
        wait("g", "pe", cnt["pe"])
        qc.dma(g8sb[:], cc_out[i])
        tok_g8 = qc.close()
        emit_rpass(0)
        emit_sigs(0)
        emit_rpass(1)

        qc.wait("t", tok_g8)
        wait("t", "v", t_ccrow)
        emit("t", lambda e: e.matmul(P[4][0:1, :], sels[:], g8sb[:, 0:512],
                                     start=True, stop=True), "pe")
        emit("t", lambda e: e.matmul(P[5][0:1, :], sels[:], g8sb[:, 512:1024],
                                     start=True, stop=True), "pe")
        wait("v", "pe", cnt["pe"])
        emit("v", lambda e: e.tensor_copy(rowb[0:1, 0:512], P[4][0:1, :]), "v")
        t_carow = emit("v", lambda e: e.tensor_copy(rowb[0:1, 512:1024],
                                                    P[5][0:1, :]), "v")
        wait("t", "v", t_carow)
        for ec in range(ECN):
            emit("t", lambda e, o=P[4][:, ec:ec + 1],
                 i2=rowb[0:1, ec * 128:(ec + 1) * 128]:
                 e.transpose(o, i2, idf[0:1, 0:1]), "pe")
        wait("v", "pe", cnt["pe"])
        t_carry = emit("v", lambda e: e.tensor_copy(ccarry[:], P[4][:, 0:8]),
                       "v")
        if i == 0:
            tap("ca0", ccarry, F32, 8)

        for ec in range(ECN):
            if ec >= 1:
                ld_wo(ec)
            srcv = BIG2[:, ec * TC:(ec + 1) * TC]
            wait("v", "v", cnt["v"])
            emit("v", lambda e, s2=srcv, cc2=ec: e.tensor_tensor_scan(
                SC[:], _b0(dec1[:, 0:1], TC), s2,
                ccarry[:, cc2:cc2 + 1], ALU.mult, ALU.add), "v")
            if ec >= 2:
                emit_rpass(ec)
            if ec + 2 < ECN:
                ld_r(ec + 2)
            if ec >= 1:
                emit_sigs(ec)
            for t5 in range(NT5):
                wait("v", "s", t_sigs[ec][t5])
                ydst = BIG4[:, ec * TC + t5 * 512: ec * TC + (t5 + 1) * 512]
                emit("v", lambda e, d2=ydst, sg=SIG[t5], t5_=t5: e.tensor_mul(
                    d2, sg, SC[:, t5_ * 512:(t5_ + 1) * 512]), "v")
                t_y_ring[t5] = cnt["v"]
        t_y_done = cnt["v"]
        if i == 0:
            tap("y0", BIG4, BF16, 16384)

        for ec in range(ECN):
            qw.wait("t", tok_wo[ec])
        wait("t", "v", t_y_done)
        XR = [xre, rsl2]
        tok_xr2 = [None, None]
        if i == 1:
            wait("g", "s", t_aff[-1])
            wait("g", "v", t_y_done)
            for pre in range(2):
                qx.open()
                qx.dma(XR[pre][:], x_in[pre * 128:(pre + 1) * 128, :])
                tok_xr2[pre] = qx.close()
            wait("v", "s", cnt["s"])
        t_ev_ring = [0, 0]
        for tb in range(NTB):
            par = tb % 2
            pa, pb = (P[4], P[5]) if par == 0 else (P[0], P[1])
            wait("t", "v", t_ev_ring[par])
            for ec in range(ECN):
                emit("t", lambda e, o=pa[:],
                     l2=BIG4[:, ec * TC + tb * 128: ec * TC + (tb + 1) * 128],
                     r2=wobA[:, ec * 512:(ec + 1) * 512],
                     a=ec == 0, b=ec == ECN - 1:
                     e.matmul(o, l2, r2, start=a, stop=b), "pe")
            for ec in range(ECN):
                emit("t", lambda e, o=pb[:],
                     l2=BIG4[:, ec * TC + tb * 128: ec * TC + (tb + 1) * 128],
                     r2=wobB[:, ec * 512:(ec + 1) * 512],
                     a=ec == 0, b=ec == ECN - 1:
                     e.matmul(o, l2, r2, start=a, stop=b), "pe")
            t_mm = cnt["pe"]
            if i == 1:
                qx.wait("v", tok_xr2[tb % 2])
                sl = BIG1[:, tb * D:(tb + 1) * D]
                emit("v", lambda e, s2=sl, xr2=XR[tb % 2]:
                     e.scalar_tensor_tensor(
                         s2, xr2[:], 1.0, s2, ALU.mult, ALU.add), "v")
                if tb + 2 < NTB:
                    wait("g", "v", cnt["v"])
                    qx.open()
                    qx.dma(XR[tb % 2][:], x_in[(tb + 2) * 128:(tb + 3) * 128, :])
                    tok_xr2[tb % 2] = qx.close()
            wait("v", "pe", t_mm)
            sl0 = BIG1[:, tb * D: tb * D + 512]
            sl1 = BIG1[:, tb * D + 512: tb * D + 1024]
            emit("v", lambda e, s2=sl0, p2=pa: e.tensor_add(s2, s2, p2[:]), "v")
            emit("v", lambda e, s2=sl1, p2=pb: e.tensor_add(s2, s2, p2[:]), "v")
            t_ev_ring[par] = cnt["v"]
            if i == 1:
                sl = BIG1[:, tb * D:(tb + 1) * D]
                emit("v", lambda e, s2=sl, o=ST["sums"][:, tb:tb + 1]:
                     e.reduce_sum(o, s2, axis=AXX), "v")
                wait("s", "v", cnt["v"])
                wait("s", "s", cnt["s"])
                emit("s", lambda e, s2=sl, o=ST["sumsq"][:, tb:tb + 1]:
                     e.activation(SC[:, 0:1024], s2, AF.Square, accum_out=o),
                     "s")
        t_wo_end = cnt["pe"]
        if i == 0:
            tap("s0", BIG1, F32, 16384)

    tap("x1", BIG1, F32, 16384)
    t_big1_final = cnt["v"]
    wcfh = wcf.rearrange("(h l5 p) d -> h p l5 d", h=2, l5=16, p=128)
    qo2.wait("g", tok_f)
    wait("g", "pe", t_wo_end)
    qw.open()
    qw.dma(WCONB[:, :].rearrange("p (l5 d) -> p l5 d", l5=16), wcfh[1])
    g_wcb = qw.close()
    wait("s", "v", t_big1_final)
    qo2.open()
    qo2.dma(x1d[:, 0:8192], BIG1[:, 0:8192])
    tok_x1d = qo2.close()
    wait("y", "v", t_big1_final)
    qy.open()
    qy.dma(x1d[:, 8192:16384], BIG1[:, 8192:16384])
    tok_x1d2 = qy.close()
    t_aff = ln_stats_finish(BIG1, FSTN)
    transpose_normed(t_aff, layer=None, src=FSTN)
    wait("g", "v", cnt["v"])
    wait("g", "s", cnt["s"])
    qo.open()
    qo.dma(n2td, BIG3[:, :])
    tok_spill = qo.close()
    barrier()

    qo.wait("y", tok_spill)
    qo.wait("v", tok_spill)
    qo.wait("g", tok_spill)
    qo.wait("s", tok_spill)
    qo2.wait("y", tok_x1d)
    qo2.wait("g", tok_x1d)
    qo2.wait("v", tok_f)
    qo.wait("y", tok_m1)
    qo.wait("g", tok_m1)
    qo2.wait("y", tok_uh)
    qo2.wait("y", tok_f)
    qo2.wait("s", tok_f)
    qy.wait("y", tok_x1d2)
    qy.wait("g", tok_x1d2)
    qy.open()
    for dc in range(4):
        qy.dma(WUP[:, dc * H:(dc + 1) * H], m1p[dc])
    qy.dma(UH[:], uhp)
    qy.dma(bcon_sb[:], bcon2)
    g_wup = qy.close()
    qw.open()
    for dc in range(4, DCN):
        qw.dma(WUP[:, dc * H:(dc + 1) * H], m1p[dc])
    g_wup2 = qw.close()
    qo2.open()
    qo2.dma(WCONA[:, :].rearrange("p (l5 d) -> p l5 d", l5=16), wcfh[0])
    g_wca = qo2.close()
    qy.wait("v", g_wup)
    emit("v", lambda e: e.memset(ones2f[:], 1.0), "v")
    t_ones = cnt["v"]
    qy.wait("t", g_wup)
    qw.wait("t", g_wup2)
    qo2.wait("t", g_wca)
    qw.wait("t", g_wcb)
    wait("t", "v", t_ones)

    XR2 = [xre, rsl2]
    tok_fx = [None, None]
    wait("g", "v", t_big1_final)
    wait("g", "s", t_aff[-1])
    for pre in range(2):
        qx.open()
        qx.dma(XR2[pre][:], x1d[:, pre * D:(pre + 1) * D])
        tok_fx[pre] = qx.close()

    st_toks = []
    t_cev = 0
    t_cev_ring = [0, 0]
    t_h1_ring = [0, 0]
    t_gelu_ring = [0, 0]
    t_exp_end = 0
    t_con_end = 0
    for t5 in range(NT5):
        if t5 >= 1:
            wait("y", "pe", t_exp_end)
        qy.open()
        for dc in range(DCN):
            qy.dma(N2TS[:, dc * 512:(dc + 1) * 512],
                   n2td[:, dc * TC + t5 * 512: dc * TC + (t5 + 1) * 512])
        tok_n2 = qy.close()
        qy.wait("t", tok_n2)
        t_gelu = []

        def emit_expand(l5):
            m = l5 % 2
            pex = P[0] if m == 0 else P[1]
            wait("t", "v", t_h1_ring[m])
            for dc in range(DCN):
                emit("t", lambda e, p2=pex,
                     l2=WUP[:, dc * H + l5 * 128: dc * H + (l5 + 1) * 128],
                     r2=N2TS[:, dc * 512:(dc + 1) * 512],
                     a=dc == 0, b=dc == DCN - 1:
                     e.matmul(p2[:], l2, r2, start=a, stop=b), "pe")
            return cnt["pe"]

        def emit_h1(l5, t_exp):
            m = l5 % 2
            pex = P[0] if m == 0 else P[1]
            wait("v", "pe", t_exp)
            t_h1 = emit("v", lambda e, p2=pex, h2=m * 512, l2=l5: e.tensor_add(
                h1buf[:, h2:h2 + 512], p2[:],
                _b0(b1sb[:, l2:l2 + 1], 512)), "v")
            t_h1_ring[m] = t_h1
            return t_h1

        def emit_ghigh_gelu(l5, t_h1, t5_):
            m = l5 % 2
            pgh = P[2] if m == 0 else P[3]
            wait("t", "v", t_h1)
            wait("t", "s", t_gelu_ring[m])
            wait("t", "v", t_cev_ring[1])
            emit("t", lambda e, l2=UH[:, l5 * 128:(l5 + 1) * 128],
                 r2=h1buf[:, m * 512:m * 512 + 512], p2=pgh:
                 e.matmul(p2[:], l2, r2, start=True, stop=True), "pe")
            t_gh = cnt["pe"]
            wait("s", "pe", t_gh)
            if t5_ >= 1:
                wait("s", "pe", t_con_end)
            t_gelu.append(emit("s", lambda e, p2=pgh, l2=l5: e.activation(
                GBUF[:, l2 * 512:(l2 + 1) * 512], p2[:], AF.Gelu,
                bias=upb_sb[:, l2:l2 + 1]), "s"))
            t_gelu_ring[m] = t_gelu[-1]

        th_prev = None
        for l5 in range(32):
            t_exp = emit_expand(l5)
            t_exp_end = t_exp
            if l5 >= 1:
                emit_ghigh_gelu(l5 - 1, th_prev, t5)
            th_prev = emit_h1(l5, t_exp)
        emit_ghigh_gelu(31, th_prev, t5)
        t_gelu_all = t_gelu[-1]

        for tq in range(4):
            tbg = t5 * 4 + tq
            pa, pb = (P[4], P[5]) if tq % 2 == 0 else (P[2], P[3])
            wait("t", "s", t_gelu_all)
            wait("t", "v", t_cev_ring[tq % 2])
            if tq % 2 == 1:
                wait("t", "s", t_gelu_all)
            emit("t", lambda e, p2=pa: e.matmul(p2[:], ones2f[:],
                                                bcon_sb[:, 0:512], start=True,
                                                stop=False), "pe")
            emit("t", lambda e, p2=pb: e.matmul(p2[:], ones2f[:],
                                                bcon_sb[:, 512:1024],
                                                start=True, stop=False), "pe")
            for l5 in range(32):
                lh = GBUF[:, l5 * 512 + tq * 128: l5 * 512 + (tq + 1) * 128]
                wt = WCONA if l5 < 16 else WCONB
                emit("t", lambda e, l2=lh, p2=pa,
                     r2=wt[:, (l5 % 16) * 1024:(l5 % 16) * 1024 + 512],
                     b=l5 == 31:
                     e.matmul(p2[:], l2, r2, start=False, stop=b), "pe")
                emit("t", lambda e, l2=lh, p2=pb,
                     r2=wt[:, (l5 % 16) * 1024 + 512:(l5 % 16) * 1024 + 1024],
                     b=l5 == 31:
                     e.matmul(p2[:], l2, r2, start=False, stop=b), "pe")
            t_con_end = cnt["pe"]
            qx.wait("v", tok_fx[tbg % 2])
            wait("v", "pe", t_con_end)
            off = (tbg % 2) * 1024
            if len(st_toks) >= 2:
                qs.wait("v", st_toks[-2])
            emit("v", lambda e, o2=off, xr=XR2[tbg % 2], p2=pa: e.tensor_add(
                OSTG[:, o2:o2 + 512], xr[:, 0:512], p2[:]), "v")
            t_ost = emit("v", lambda e, o2=off, xr=XR2[tbg % 2], p2=pb:
                         e.tensor_add(OSTG[:, o2 + 512:o2 + 1024],
                                      xr[:, 512:1024], p2[:]), "v")
            wait("g", "v", t_ost)
            qs.open()
            qs.dma(y_out[tbg * 128:(tbg + 1) * 128, :], OSTG[:, off:off + 1024])
            st_toks.append(qs.close())
            if tbg + 2 < 16:
                qx.open()
                qx.dma(XR2[tbg % 2][:], x1d[:, (tbg + 2) * D:(tbg + 3) * D])
                tok_fx[tbg % 2] = qx.close()
            t_cev_ring[tq % 2] = t_ost
            t_cev = t_ost

    qs.wait_all("g")

    with nc.Block() as block:
        @block.tensor
        def _(e):
            for f in ES["t"]:
                f(e)

        @block.vector
        def _(e):
            for f in ES["v"]:
                f(e)

        @block.scalar
        def _(e):
            for f in ES["s"]:
                f(e)

        @block.gpsimd
        def _(e):
            for f in ES["g"]:
                f(e)

        @block.sync
        def _(e):
            for f in ES["y"]:
                f(e)

    return nc


def _dense_low(w, s, transpose):
    out = np.zeros((32, 128, 128), np.float32)
    s_ = 1 << s
    i = np.arange(4096)
    i = i[(i & s_) == 0]
    k = ((i >> (s + 1)) << s) | (i & (s_ - 1))
    c = i >> 7
    r = i & 127
    r2 = r + s_
    if not transpose:
        out[c, r, r] = w[k, 0, 0]
        out[c, r, r2] = w[k, 0, 1]
        out[c, r2, r] = w[k, 1, 0]
        out[c, r2, r2] = w[k, 1, 1]
    else:
        out[c, r, r] = w[k, 0, 0]
        out[c, r2, r] = w[k, 0, 1]
        out[c, r, r2] = w[k, 1, 0]
        out[c, r2, r2] = w[k, 1, 1]
    return out


def _dense_high(w, s, transpose):
    out = np.zeros((32, 128, 128), np.float32)
    sb = 1 << (s - 7)
    cc = np.arange(32)
    cc = cc[(cc & sb) == 0]
    cc2 = cc + sb
    if not transpose:
        for l5 in range(32):
            for b56 in range(4):
                jA = cc * 128 + b56 * 32 + l5
                kA = ((jA >> (s + 1)) << s) | (jA & ((1 << s) - 1))
                out[l5, b56 * 32 + cc, b56 * 32 + cc] = w[kA, 0, 0]
                out[l5, b56 * 32 + cc, b56 * 32 + cc2] = w[kA, 0, 1]
                out[l5, b56 * 32 + cc2, b56 * 32 + cc] = w[kA, 1, 0]
                out[l5, b56 * 32 + cc2, b56 * 32 + cc2] = w[kA, 1, 1]
    else:
        for gl in range(32):
            for m in range(4):
                l = 4 * gl + m
                jA = cc * 128 + l
                kA = ((jA >> (s + 1)) << s) | (jA & ((1 << s) - 1))
                out[gl, m * 32 + cc, m * 32 + cc] = w[kA, 0, 0]
                out[gl, m * 32 + cc2, m * 32 + cc] = w[kA, 0, 1]
                out[gl, m * 32 + cc, m * 32 + cc2] = w[kA, 1, 0]
                out[gl, m * 32 + cc2, m * 32 + cc2] = w[kA, 1, 1]
    return out


def kernel(L0_attn_h, L0_attn_ln_w, L0_attn_ln_b, L0_attn_Wk, L0_attn_Wv,
           L0_attn_Wr, L0_attn_Wo, L0_ffn_W_exp, L0_ffn_b_exp,
           L0_ffn_up_weights, L0_ffn_up_bias, L0_ffn_down_weights,
           L0_ffn_W_con, L0_ffn_b_con, L0_ffn_down_bias):
    f32 = np.float32
    x = np.ascontiguousarray(np.asarray(L0_attn_h, f32))
    Wk = np.asarray(L0_attn_Wk, f32)
    Wv = np.asarray(L0_attn_Wv, f32)
    Wr = np.asarray(L0_attn_Wr, f32)
    Wo = np.asarray(L0_attn_Wo, f32)
    ln_w = np.asarray(L0_attn_ln_w, f32)
    ln_b = np.asarray(L0_attn_ln_b, f32)
    We = np.asarray(L0_ffn_W_exp, f32)
    b_exp = np.asarray(L0_ffn_b_exp, f32)
    upw = np.asarray(L0_ffn_up_weights, f32)
    upb = np.asarray(L0_ffn_up_bias, f32)
    dnw = np.asarray(L0_ffn_down_weights, f32)
    Wc = np.asarray(L0_ffn_W_con, f32)
    b_con = np.asarray(L0_ffn_b_con, f32)
    dnb = np.asarray(L0_ffn_down_bias, f32)

    def proj_tiles(W):
        WT = np.ascontiguousarray(W.T)
        return np.ascontiguousarray(
            WT.reshape(8, 128, 8, 128).transpose(2, 1, 0, 3).reshape(8, 128, D))

    def wo_tiles(W):
        return np.ascontiguousarray(W.T.reshape(8, 128, D))

    com = {
        "ident": np.eye(128, dtype=f32),
        "wexpT": np.ascontiguousarray(
            We.T.reshape(32, 128, 8, 128).transpose(2, 1, 0, 3)
            .reshape(8, 128, 4096)),
        "wconP": np.ascontiguousarray(
            Wc.reshape(32, 128, D).transpose(1, 0, 2).reshape(32, 128, D)
            .reshape(8, 4, 128, 1024).transpose(0, 2, 1, 3)
            .reshape(8, 128, 4096)),
        "dsul": np.ascontiguousarray(
            np.stack([_dense_low(upw[s], s, False) for s in range(7)])
            .transpose(0, 2, 1, 3).reshape(7, 128, 4096)),
        "dsuh": np.ascontiguousarray(
            np.stack([_dense_high(upw[s], s, False) for s in range(7, 12)])
            .transpose(0, 2, 1, 3).reshape(5, 128, 4096)),
        "dsdl": np.ascontiguousarray(
            np.stack([_dense_low(dnw[s], s, True) for s in range(7)])
            .transpose(0, 2, 1, 3).reshape(7, 128, 4096)),
        "dsdh": np.ascontiguousarray(
            np.stack([_dense_high(dnw[s], s, True) for s in range(7, 12)])
            .transpose(0, 2, 1, 3).reshape(5, 128, 4096)),
        "bexp_c": np.ascontiguousarray(b_exp.reshape(32, 128).T),
        "upb_r": np.ascontiguousarray(
            upb.reshape(32, 4, 32).transpose(1, 0, 2).reshape(128, 32)),
        "bcon2": np.ascontiguousarray(np.stack([b_con, dnb])),
    }
    pid = np.zeros((128, 128), f32)
    jn = np.arange(128)
    pid[(jn % 4) * 32 + jn // 4, jn] = 1.0
    com["pident"] = pid
    com["pidT"] = np.ascontiguousarray(pid.T)
    for i in range(2):
        com[f"wk{i}"] = proj_tiles(Wk[i])
        com[f"wv{i}"] = proj_tiles(Wv[i])
        com[f"wr{i}"] = proj_tiles(Wr[i])
        com[f"wo{i}"] = wo_tiles(Wo[i])
        com[f"lnw{i}"] = np.ascontiguousarray(ln_w[i].reshape(8, 128).T)
        com[f"lnb{i}"] = np.ascontiguousarray(ln_b[i].reshape(8, 128).T)

    nc = build_nc()
    in_maps = []
    for c in range(8):
        b, h = c // 2, c % 2
        selv = np.zeros((8, 1), f32)
        if h == 1:
            selv[c - 1, 0] = 1.0
        im = dict(com)
        im["x_in"] = np.ascontiguousarray(x[b, h * TC:(h + 1) * TC, :])
        im["sel"] = selv
        in_maps.append(im)

    res = run_bass_kernel_spmd(nc, in_maps, core_ids=list(range(8)))
    global LAST_RES
    LAST_RES = res.results
    out = np.empty((4, 4096, D), f32)
    for c in range(8):
        b, h = c // 2, c % 2
        out[b, h * TC:(h + 1) * TC, :] = res.results[c]["y_out"]
    return out
